# version 28
# baseline (speedup 1.0000x reference)
import math
import os
import numpy as np
import ml_dtypes
import concourse.bass as bass
import concourse.mybir as mybir
from concourse.bass_utils import run_bass_kernel_spmd

F32 = mybir.dt.float32
BF16 = mybir.dt.bfloat16
I32 = mybir.dt.int32
AF = mybir.ActivationFunctionType
ALU = mybir.AluOpType
AX = mybir.AxisListType

_DBG = bool(int(os.environ.get("MK_DBG", "0")))
D = 2048
T = 4096
H = 16
ITER = 5504
NIC = ITER // 128
NHC = D // 128
RS2 = 1.0 / math.sqrt(128.0)
LN_EPS = 1e-5

ENGINES = ("sync", "scalar", "vector", "tensor", "gpsimd")
ENG_SEM = {"tensor": "PE", "scalar": "AC", "vector": "DV", "gpsimd": "PL"}
COMPUTE_SEMS = {"PE", "AC", "DV", "PL"}


class Sched:

    FORCE_SYNC_OUT = bool(int(os.environ.get("MK_SYNCALL", "1")))

    DQ = {"sync": ("LD", 16), "scalar": ("LS", 4), "gpsimd": ("LG", 4)}

    def __init__(self, sems):
        self.sems = sems
        self.q = {e: [] for e in ENGINES}
        self.cnt = {s: 0 for s in sems}
        self.res = {}
        self.hw = {}
        self.alias = {}
        self.dma_rot = {e: 0 for e in self.DQ}
        self.dma_grp = {}

    def _wait(self, eng, sem, val):
        if val <= 0:
            return
        key = (eng, sem)
        if self.hw.get(key, 0) >= val:
            return
        self.hw[key] = val
        s = self.sems[sem]
        self.q[eng].append(("w", sem, val,
                            lambda e, s=s, v=val: e.wait_ge(s, v)))

    def raw(self, eng, fn):
        self.q[eng].append(("o", None, None, fn))

    def _entry_wait(self, eng, entry, is_dma, war=False):
        sem, val, peng = entry
        if (not is_dma) and (not war) and peng == eng and sem in COMPUTE_SEMS:
            return
        if is_dma and peng == eng and sem not in COMPUTE_SEMS:
            return
        self._wait(eng, sem, val)

    def _pre(self, eng, reads, writes, is_dma):
        for r in writes:
            for stale in self.alias.pop(r, ()):
                st = self.res.get(stale)
                if st:
                    if st["w"]:
                        self._entry_wait(eng, st["w"], is_dma)
                    for e2 in st["r"].values():
                        self._entry_wait(eng, e2, is_dma, war=True)
        for r in reads:
            st = self.res.get(r)
            if st and st["w"]:
                self._entry_wait(eng, st["w"], is_dma)
        for r in writes:
            st = self.res.get(r)
            if st:
                if st["w"]:
                    self._entry_wait(eng, st["w"], is_dma)
                for e2 in st["r"].values():
                    self._entry_wait(eng, e2, is_dma, war=True)

    def _post(self, eng, reads, writes, sem, val, sync_out=False):
        if self.FORCE_SYNC_OUT:
            sync_out = True
        for r in writes:
            self.res[r] = {"w": (sem, val, None if sync_out else eng), "r": {}}
        for r in reads:
            st = self.res.setdefault(r, {"w": None, "r": {}})
            k = (sem, eng)
            if st["r"].get(k, (None, 0, None))[1] < val:
                st["r"][k] = (sem, val, eng)

    def cop(self, eng, fn, reads=(), writes=(), sem=None, k=1,
            sync_out=False):
        sem = sem or ENG_SEM[eng]
        self._pre(eng, reads, writes, False)
        self.cnt[sem] += k
        val = self.cnt[sem]
        s = self.sems[sem]
        self.q[eng].append(("i", sem, k,
                            lambda e, fn=fn, s=s, k=k: fn(e).then_inc(s, k)))
        self._post(eng, reads, writes, sem, val, sync_out)
        return val

    def grp(self, eng, fns, reads=(), writes=()):
        sem = ENG_SEM[eng]
        self._pre(eng, reads, writes, False)
        for fn in fns[:-1]:
            self.raw(eng, fn)
        self.cnt[sem] += 1
        val = self.cnt[sem]
        s = self.sems[sem]
        self.q[eng].append(("i", sem, 1,
                            lambda e, fn=fns[-1], s=s: fn(e).then_inc(s, 1)))
        self._post(eng, reads, writes, sem, val)
        return val

    def dma(self, eng, out, in_, sem=None, reads=(), writes=(), war=(),
            grp=None):
        base, M = self.DQ[eng]
        if grp is None or grp not in self.dma_grp:
            i = self.dma_rot[eng] % M
            self.dma_rot[eng] += 1
            psem = f"{base}{i}"
            if grp is not None:
                self.dma_grp[grp] = psem
            self._wait(eng, psem, self.cnt[psem])
        else:
            psem = self.dma_grp[grp]
        self._pre(eng, reads, tuple(writes) + tuple(war), True)
        self.cnt[psem] += 16
        val = self.cnt[psem]
        s = self.sems[psem]
        self.q[eng].append(("i", psem, 16,
                            lambda e, out=out, in_=in_, s=s:
                            e.dma_start(out=out, in_=in_).then_inc(s, 16)))
        self._post(eng, reads, writes, psem, val)
        return psem, val

    def drain_dmas(self, eng_wait="scalar", queues=("sync", "scalar",
                                                    "gpsimd")):
        for qeng in queues:
            base, M = self.DQ[qeng]
            for i in range(M):
                psem = f"{base}{i}"
                self._wait(eng_wait, psem, self.cnt[psem])

    def verify(self):
        pos = {e: 0 for e in ENGINES}
        vals = {s: 0 for s in self.sems}
        progress = True
        while progress:
            progress = False
            for e in ENGINES:
                while pos[e] < len(self.q[e]):
                    kind, sem, v, _ = self.q[e][pos[e]]
                    if kind == "w" and vals[sem] < v:
                        break
                    if kind == "i":
                        vals[sem] += v
                    pos[e] += 1
                    progress = True
        stuck = {e: (pos[e], len(self.q[e])) for e in ENGINES
                 if pos[e] < len(self.q[e])}
        if stuck:
            detail = {}
            for e in stuck:
                kind, sem, v, _ = self.q[e][pos[e]]
                detail[e] = (pos[e], kind, sem, v,
                             vals.get(sem) if sem else None)
            raise RuntimeError(f"schedule deadlock: {detail}")


def build_nc():
    nc = bass.Bass("TRN2", target_bir_lowering=False)

    xq = nc.dram_tensor("xq", [2048, D], F32, kind="ExternalInput")
    xo = nc.dram_tensor("xo", [2048, D], F32, kind="ExternalInput")
    wqT = nc.dram_tensor("wqT", [D, D], F32, kind="ExternalInput")
    wkvT = nc.dram_tensor("wkvT", [D, 256], F32, kind="ExternalInput")
    lutb = nc.dram_tensor("lutb", [128, 16], F32, kind="ExternalInput")
    gidx = nc.dram_tensor("gidx", [256, ITER], I32, kind="ExternalInput")
    uidx = nc.dram_tensor("uidx", [256, ITER], I32, kind="ExternalInput")
    didx = nc.dram_tensor("didx", [688, D], I32, kind="ExternalInput")
    maskb_d = nc.dram_tensor("maskb", [128, 128], BF16, kind="ExternalInput")
    y_out = nc.dram_tensor("y_out", [2048, D], F32, kind="ExternalOutput")

    g_shard = nc.dram_tensor("g_shard", [256, ITER], BF16, kind="Internal")
    u_shard = nc.dram_tensor("u_shard", [256, ITER], BF16, kind="Internal")
    d_shard = nc.dram_tensor("d_shard", [688, D], BF16, kind="Internal")
    g_full = nc.dram_tensor("g_full", [D, ITER], BF16, kind="Internal", addr_space="Shared")
    u_full = nc.dram_tensor("u_full", [D, ITER], BF16, kind="Internal", addr_space="Shared")
    d_full = nc.dram_tensor("d_full", [ITER, D], BF16, kind="Internal", addr_space="Shared")
    x2_scr = nc.dram_tensor("x2_scr", [16, 128 * D], F32, kind="Internal")
    ln2_scr = nc.dram_tensor("ln2_scr", [16, 128 * 2048], BF16, kind="Internal")
    dbg = {}
    dstats = None
    if _DBG:
        dstats = nc.alloc_sbuf_tensor("dstats", [128, 3 * 48], F32)
        for nm, shape, dty in (("d_ln1o", [128, NHC * D], BF16),
                               ("d_K", [128, T], BF16),
                               ("d_V", [128, T], BF16),
                               ("d_kmax2", [128, 1], F32),
                               ("d_qp0", [128, D], BF16),
                               ("d_wqb0", [128, D], BF16),
                               ("d_negb0", [128, 4], F32),
                               ("d_denT0", [128, 16], F32),
                               ("d_yfm0", [128, D], BF16),
                               ("d_ytm", [128, 16 * D], BF16),
                               ("d_x2", [16, 128 * D], F32),
                               ("d_ln2", [16, 128 * 2048], BF16),
                               ("d_gfull", [128, ITER], BF16),
                               ("d_dfull", [128, D], BF16),
                               ("d_hact0", [128, 4 * ITER], BF16),
                               ("d_stats", [128, 3 * 48], F32)):
            dbg[nm] = nc.dram_tensor(nm, shape, dty, kind="ExternalOutput")

    tri = np.zeros((128, 128), dtype=np.float32)
    tri[np.triu_indices(128)] = 1.0
    ident_b = nc.inline_tensor(np.eye(128, dtype=ml_dtypes.bfloat16), name="c_ib")
    ident_f = nc.inline_tensor(np.eye(128, dtype=np.float32), name="c_if")
    tril_b = nc.inline_tensor(tri.astype(ml_dtypes.bfloat16), name="c_tri")
    ones_b = nc.inline_tensor(np.ones((128, 128), dtype=ml_dtypes.bfloat16),
                              name="c_on")

    def sb(name, shape, dtype):
        return nc.alloc_sbuf_tensor(name, shape, dtype)

    ident = sb("ident", [128, 128], BF16)
    identf = sb("identf", [128, 128], F32)
    tril = sb("tril", [128, 128], BF16)
    maskb = sb("maskb_s", [128, 128], BF16)
    onesb = sb("onesb_s", [128, 128], BF16)
    lutbs = sb("lutbs", [128, 16], F32)
    K = sb("K", [128, T], BF16)
    V = sb("V", [128, T], BF16)

    Qp = sb("Qp", [128, 2 * D], BF16)
    wqb = sb("wqb", [128, 2 * D], BF16)
    sqsc = sb("sqsc", [128, 512], BF16)
    PTb = sb("PTb", [128, 8 * 512], BF16)
    denrb = sb("denrb", [128, 512], BF16)
    denTr = sb("denTr", [128, 16], F32)
    denT = sb("denT", [128, 16], F32)
    yfm = sb("yfm", [128, D], BF16)
    kqmax = sb("kqmax", [128, 8], F32)
    kmax2 = sb("kmax2", [128, 1], F32)
    qmax2 = sb("qmax2", [128, 1], F32)
    negb = sb("negb", [128, 4], F32)
    t0v = sb("t0v", [128, 4], F32)
    lnkv = sb("lnkv", [128, 1], F32)
    s1v = sb("s1v", [128, 2], F32)
    s2v = sb("s2v", [128, 2], F32)
    mvec = sb("mvec", [128, 2], F32)
    vvar = sb("vvar", [128, 2], F32)
    varl = sb("varl", [128, 2], F32)
    rstd = sb("rstd", [128, 2], F32)
    nmr = sb("nmr", [128, 2], F32)
    msqn = sb("msqn", [128, 1], F32)
    epsv = sb("epsv", [128, 1], F32)
    upsc = sb("upsc", [128, 512], BF16)
    bidx = sb("bidx", [128, 2 * 512], I32)
    bidxb = sb("bidxb", [128, 512], BF16)
    bt = sb("bt", [128, 512], BF16)
    bacc = sb("bacc", [128, 2 * 512], BF16)
    bacc2 = sb("bacc2", [128, 2 * 512], BF16)
    bt2 = sb("bt2", [128, 512], BF16)

    base = (nc.sbuf_base + 31) & ~31
    R1 = base
    R2 = R1 + 65536
    assert R2 + 65536 <= 229344 - 64, f"SBUF overflow {R2 + 65536}"

    def sba(name, shape, dtype, off):
        return nc.alloc_sbuf_tensor_at(name, shape, dtype, offset=off)

    ln1o = sba("ln1o", [128, NHC * D], BF16, R1)
    xin2 = sba("xin2", [128, 2 * D], F32, R1)
    x2t = sba("x2t", [128, 2 * D], F32, R1 + 16384)
    ln2tm = sba("ln2tm", [128, 2 * D], BF16, R1 + 32768)
    ln2sb = sba("ln2sb", [128, 2 * NHC * 128], BF16, R1 + 40960)
    xsq2 = sba("xsq2", [128, D], BF16, R1 + 49152)
    urhs = sba("urhs", [128, NHC * 512], BF16, R1)
    ln2ts = sba("ln2ts", [128, NHC * 512], BF16, R1 + 16384)
    x2rl = sba("x2rl", [128, 2 * D], F32, R1 + 32768)
    wdrs = sba("wdrs", [128, 2 * D], BF16, R1 + 49152)
    xin = sba("xin", [128, 2 * D], F32, R2)
    xsq = sba("xsq", [128, D], BF16, R2 + 16384)
    lnt = sba("lnt", [128, 2 * D], BF16, R2 + 20480)
    lno = sba("lno", [128, NHC * 256], BF16, R2 + 28672)
    wkvb = sba("wkvb", [128, NHC * 256], BF16, R2 + 36864)
    ytm = sba("ytm", [128, 16 * D], BF16, R2)
    hact = sba("hact", [128, 4 * ITER], BF16, R2)
    grhs = sba("grhs", [128, NHC * 512], BF16, R2 + 44032)

    Psc = nc.alloc_psum_tensor("Psc", [128, 4 * 512], F32)
    P = {}
    for name in ("Py", "Pd", "T0", "T1"):
        P[name] = nc.alloc_psum_tensor(name, [128, 512], F32)
    BANKS8 = ("B0", "B1", "B2", "B3", "Py", "Pd", "T0", "T1")

    def bap(name):
        if name.startswith("B"):
            i = int(name[1])
            return Psc[:, 512 * i:512 * (i + 1)]
        return P[name][:, :]

    def tbank_bf16(name):
        return P[name].ap().bitcast(BF16)

    sems = {}
    names = ["PE", "AC", "DV", "PL", "C0", "C1", "C2"]
    names += [f"LD{i}" for i in range(16)]
    names += [f"LS{i}" for i in range(4)]
    names += [f"LG{i}" for i in range(4)]
    for s in names:
        sems[s] = nc.alloc_semaphore(name=s)
    S = Sched(sems)

    for dst, src, rs in ((ident, ident_b, "ident"), (identf, ident_f, "identf"),
                         (tril, tril_b, "tril"), (maskb, maskb_d, "maskb"),
                         (onesb, ones_b, "onesb"), (lutbs, lutb, "lutbs")):
        S.dma("sync", dst[:, :], src.ap(), writes=(rs,))
    for hc in range(NHC):
        S.dma("gpsimd", wkvb[:, 256 * hc:256 * (hc + 1)],
              wkvT[128 * hc:128 * (hc + 1), :], grp="wkv",
              writes=("wkvb",) if hc == NHC - 1 else ())

    tiles = []
    for (src, dst, nrows, ncols) in ((gidx, g_shard, 256, ITER),
                                     (uidx, u_shard, 256, ITER),
                                     (didx, d_shard, 688, D)):
        p0 = 0
        while p0 < nrows:
            pr = min(128, nrows - p0)
            c0 = 0
            while c0 < ncols:
                cw = min(512, ncols - c0)
                tiles.append((src, dst, p0, pr, c0, cw))
                c0 += cw
            p0 += pr

    NPOOL_L = 0

    def emit_build_tile_steps(i):
        src_, dst, p0, pr, c0, cw = tiles[i]
        sl = i % 2
        S.dma("sync", bidx[0:pr, 512 * sl:512 * sl + cw],
              src_[p0:p0 + pr, c0:c0 + cw], writes=(f"bidx{sl}",))
        S.cop("vector", lambda e, pr=pr, cw=cw, sl=sl: e.tensor_copy(
            bidxb[0:pr, 0:cw], bidx[0:pr, 512 * sl:512 * sl + cw]),
            reads=(f"bidx{sl}", "lutbs"), writes=("bidxb",))
        yield
        for l in range(16 - NPOOL_L):
            if l == 0:
                S.cop("vector", lambda e, pr=pr, cw=cw, sl=sl, l=l:
                      e.tensor_scalar(bacc[0:pr, 512 * sl:512 * sl + cw],
                                      bidxb[0:pr, 0:cw], float(l),
                                      lutbs[0:pr, l:l + 1],
                                      ALU.is_equal, ALU.mult),
                      reads=("bidxb",), writes=(f"bacc{sl}",))
            else:
                S.cop("vector", lambda e, pr=pr, cw=cw, l=l:
                      e.tensor_scalar(bt[0:pr, 0:cw], bidxb[0:pr, 0:cw],
                                      float(l), lutbs[0:pr, l:l + 1],
                                      ALU.is_equal, ALU.mult),
                      reads=("bidxb",), writes=("bt",))
                S.cop("vector", lambda e, pr=pr, cw=cw, sl=sl:
                      e.tensor_tensor(bacc[0:pr, 512 * sl:512 * sl + cw],
                                      bacc[0:pr, 512 * sl:512 * sl + cw],
                                      bt[0:pr, 0:cw], ALU.add),
                      reads=("bt", f"bacc{sl}"), writes=(f"bacc{sl}",))
            if l % 2 == 1:
                yield
        for l in range(16 - NPOOL_L, 16) if NPOOL_L else ():
            if l == 16 - NPOOL_L:
                S.cop("gpsimd", lambda e, pr=pr, cw=cw, sl=sl, l=l:
                      e.tensor_scalar(bacc2[0:pr, 512 * sl:512 * sl + cw],
                                      bidxb[0:pr, 0:cw], float(l),
                                      lutbs[0:pr, l:l + 1],
                                      ALU.is_equal, ALU.mult),
                      reads=("bidxb",), writes=(f"bacc2{sl}",))
            else:
                S.cop("gpsimd", lambda e, pr=pr, cw=cw, l=l:
                      e.tensor_scalar(bt2[0:pr, 0:cw], bidxb[0:pr, 0:cw],
                                      float(l), lutbs[0:pr, l:l + 1],
                                      ALU.is_equal, ALU.mult),
                      reads=("bidxb",), writes=("bt2",))
                S.cop("gpsimd", lambda e, pr=pr, cw=cw, sl=sl:
                      e.tensor_tensor(bacc2[0:pr, 512 * sl:512 * sl + cw],
                                      bacc2[0:pr, 512 * sl:512 * sl + cw],
                                      bt2[0:pr, 0:cw], ALU.add),
                      reads=("bt2", f"bacc2{sl}"), writes=(f"bacc2{sl}",))
            yield
        if NPOOL_L:
            S.cop("vector", lambda e, pr=pr, cw=cw, sl=sl:
                  e.tensor_tensor(bacc[0:pr, 512 * sl:512 * sl + cw],
                                  bacc[0:pr, 512 * sl:512 * sl + cw],
                                  bacc2[0:pr, 512 * sl:512 * sl + cw],
                                  ALU.add),
                  reads=(f"bacc2{sl}", f"bacc{sl}"), writes=(f"bacc{sl}",))
        S.dma("gpsimd", dst[p0:p0 + pr, c0:c0 + cw],
              bacc[0:pr, 512 * sl:512 * sl + cw], grp=("shard", dst.name),
              reads=(f"bacc{sl}",), writes=(dst.name,))
        yield

    def _build_gen():
        for i in range(len(tiles)):
            for step in emit_build_tile_steps(i):
                yield
        for sh, fl, csem in ((g_shard, g_full, "C0"),
                             (u_shard, u_full, "C1"),
                             (d_shard, d_full, "C2")):
            S.cop("gpsimd", lambda e, sh=sh, fl=fl: e.collective_compute(
                "AllGather", ALU.bypass, replica_groups=[list(range(8))],
                ins=[sh.ap().opt()], outs=[fl.ap().opt()]),
                reads=(sh.name,), writes=(fl.name,), sem=csem)
        while True:
            yield

    build_iter = _build_gen()

    def emit_build(n):
        for _ in range(n):
            next(build_iter)

    S.cop("vector", lambda e: e.memset(kmax2[:, :], 0.0), writes=("kmax2",))
    S.cop("vector", lambda e: e.memset(epsv[:, :], LN_EPS), writes=("epsv",))

    _dbgidx = [0]

    def ln_front(src_res, src_ap_fn, sl, sq=None, sq_res="xsq"):
        S.cop("vector", lambda e, sl=sl: e.reduce_sum(
            s1v[:, sl:sl + 1], src_ap_fn(), AX.X),
            reads=(src_res,), writes=(f"s1-{sl}",), sync_out=True)
        sq = xsq if sq is None else sq
        S.cop("scalar", lambda e, sl=sl, sq=sq: e.activation(
            sq[:, :], src_ap_fn(), AF.Square, accum_out=s2v[:, sl:sl + 1]),
            reads=(src_res,), writes=(sq_res, f"s2{sl}"))

    def ln_back(src_res, src_ap_fn, dst_ap, sl, out_res):
        S.cop("vector", lambda e, sl=sl: e.tensor_scalar(
            mvec[:, sl:sl + 1], s1v[:, sl:sl + 1], 1.0 / D, None, ALU.mult),
            reads=(f"s1-{sl}",), writes=(f"mv{sl}",), sync_out=True)
        S.cop("vector", lambda e, sl=sl: e.tensor_scalar(
            vvar[:, sl:sl + 1], s2v[:, sl:sl + 1], 1.0 / D, None, ALU.mult),
            reads=(f"s2{sl}",), writes=(f"vv{sl}",), sync_out=True)
        S.cop("vector", lambda e, sl=sl: e.scalar_tensor_tensor(
            msqn[:, :], mvec[:, sl:sl + 1], -1.0, mvec[:, sl:sl + 1],
            ALU.mult, ALU.mult), reads=(f"mv{sl}",), writes=("msqn",), sync_out=True)
        S.cop("vector", lambda e, sl=sl: e.tensor_tensor(
            vvar[:, sl:sl + 1], vvar[:, sl:sl + 1], msqn[:, :], ALU.add),
            reads=("msqn", f"vv{sl}"), writes=(f"vv{sl}",), sync_out=True)
        S.cop("scalar", lambda e, sl=sl: e.activation(
            varl[:, sl:sl + 1], vvar[:, sl:sl + 1], AF.Ln,
            bias=epsv[:, :]),
            reads=(f"vv{sl}", "epsv"), writes=(f"vl{sl}",), sync_out=True)
        S.cop("scalar", lambda e, sl=sl: e.activation(
            rstd[:, sl:sl + 1], varl[:, sl:sl + 1], AF.Exp, scale=-0.5),
            reads=(f"vl{sl}",), writes=(f"rs{sl}",), sync_out=True)
        S.cop("vector", lambda e, sl=sl: e.scalar_tensor_tensor(
            nmr[:, sl:sl + 1], mvec[:, sl:sl + 1], -1.0, rstd[:, sl:sl + 1],
            ALU.mult, ALU.mult), reads=(f"mv{sl}", f"rs{sl}"),
            writes=(f"nm{sl}",), sync_out=True)
        if _DBG:
            di = _dbgidx[0]; _dbgidx[0] += 1
            if di < 48:
                S.cop("vector", lambda e, sl=sl, di=di: e.tensor_copy(
                    dstats[:, 3 * di:3 * di + 1], mvec[:, sl:sl + 1]),
                    reads=(f"mv{sl}",), writes=("dstats",))
                S.cop("vector", lambda e, sl=sl, di=di: e.tensor_copy(
                    dstats[:, 3 * di + 1:3 * di + 2], rstd[:, sl:sl + 1]),
                    reads=(f"rs{sl}",), writes=("dstats",))
                S.cop("vector", lambda e, sl=sl, di=di: e.tensor_copy(
                    dstats[:, 3 * di + 2:3 * di + 3], nmr[:, sl:sl + 1]),
                    reads=(f"nm{sl}",), writes=("dstats",))
        S.cop("scalar", lambda e, sl=sl: e.activation(
            dst_ap, src_ap_fn(), AF.Identity, bias=nmr[:, sl:sl + 1],
            scale=rstd[:, sl:sl + 1]),
            reads=(src_res, f"nm{sl}", f"rs{sl}"), writes=(out_res,))

    def ln_stats_apply(src_res, src_ap_fn, dst_ap, sl, out_res,
                       sq=None, sq_res="xsq"):
        ln_front(src_res, src_ap_fn, sl, sq=sq, sq_res=sq_res)
        ln_back(src_res, src_ap_fn, dst_ap, sl, out_res)

    def tp16(src_res, src_base_ap_fn, dst_fn, tag):
        for g in range(4):
            tb = "T0" if g % 2 == 0 else "T1"
            fns = []
            for u in range(4):
                hc = 4 * g + u
                fns.append(lambda e, hc=hc, u=u, tb=tb: e.transpose(
                    tbank_bf16(tb)[:, 128 * u:128 * (u + 1)],
                    src_base_ap_fn(hc), ident[:, :]))
            S.grp("tensor", fns, reads=(src_res, "ident"), writes=(tb,))
            dst_ap, dst_res = dst_fn(g)
            S.cop("vector", lambda e, tb=tb, dst_ap=dst_ap: e.tensor_copy(
                dst_ap, tbank_bf16(tb)[:, 0:512]),
                reads=(tb,), writes=(dst_res,))

    def ph0_A(ti):
        j, kind = ti // 2, ("own" if ti % 2 == 0 else "oth")
        sl = ti % 2
        src = xq if kind == "own" else xo
        S.dma("sync", xin[:, D * sl:D * (sl + 1)], src[128 * j:128 * (j + 1), :],
              writes=(f"xin{sl}",))
        ln_front(f"xin{sl}",
                 lambda sl=sl: xin[:, D * sl:D * (sl + 1)], sl)

    def ph0_B(ti):
        sl = ti % 2
        ln_back(f"xin{sl}",
                lambda sl=sl: xin[:, D * sl:D * (sl + 1)],
                lnt[:, D * sl:D * (sl + 1)], sl, f"lnt{sl}")

    def ph0_C(ti):
        j, kind = ti // 2, ("own" if ti % 2 == 0 else "oth")
        sl = ti % 2
        if kind == "own":
            def dst_fn(g, j=j):
                col0 = D * (4 * g) + 128 * j
                ap = bass.AP(ln1o, col0, [[NHC * D, 128], [D, 4], [1, 128]])
                return ap, "ln1o"
        else:
            def dst_fn(g, j=j):
                col0 = 256 * (4 * g) + 128 * (j % 2)
                ap = bass.AP(lno, col0, [[NHC * 256, 128], [256, 4], [1, 128]])
                return ap, f"lno{j % 2}"
        tp16(f"lnt{sl}",
             lambda hc, sl=sl: lnt[:, D * sl + 128 * hc:D * sl + 128 * (hc + 1)],
             dst_fn, f"ln1tp{ti}")

        kt = 2 * j + (0 if kind == "own" else 1)
        kb = "B0" if ti % 2 == 0 else "B1"
        vb = "B2" if ti % 2 == 0 else "B3"
        if kind == "own":
            rhs_res = "ln1o"

            def rhs_ap(hc, j=j):
                return ln1o[:, D * hc + 128 * j:D * hc + 128 * (j + 1)]
        else:
            rhs_res = f"lno{j % 2}"

            def rhs_ap(hc, j=j):
                return lno[:, 256 * hc + 128 * (j % 2):
                           256 * hc + 128 * (j % 2) + 128]
        S.grp("tensor",
              [lambda e, hc=hc, kb=kb, rhs_ap=rhs_ap: e.matmul(
                  bap(kb)[:, 0:128], wkvb[:, 256 * hc:256 * hc + 128],
                  rhs_ap(hc), start=(hc == 0), stop=(hc == NHC - 1))
               for hc in range(NHC)],
              reads=("wkvb", rhs_res), writes=(kb,))
        S.grp("tensor",
              [lambda e, hc=hc, vb=vb, rhs_ap=rhs_ap: e.matmul(
                  bap(vb)[:, 0:128], rhs_ap(hc),
                  wkvb[:, 256 * hc + 128:256 * (hc + 1)],
                  start=(hc == 0), stop=(hc == NHC - 1))
               for hc in range(NHC)],
              reads=("wkvb", rhs_res), writes=(vb,))
        S.cop("scalar", lambda e, kt=kt, kb=kb: e.copy(
            K[:, 128 * kt:128 * (kt + 1)], bap(kb)[:, 0:128]),
            reads=(kb,), writes=("K",))
        S.cop("vector", lambda e, kt=kt, vb=vb: e.tensor_copy(
            V[:, 128 * kt:128 * (kt + 1)], bap(vb)[:, 0:128]),
            reads=(vb,), writes=("V",))
        S.cop("vector", lambda e, kt=kt: e.tensor_tensor(
            sqsc[:, 0:128], K[:, 128 * kt:128 * (kt + 1)],
            K[:, 128 * kt:128 * (kt + 1)], ALU.mult),
            reads=("K",), writes=("sqsc",))
        nb = "Py" if ti % 2 == 0 else "Pd"
        S.grp("tensor", [lambda e, nb=nb: e.matmul(
            bap(nb)[:, 0:128], onesb[:, :], sqsc[:, 0:128],
            start=True, stop=True)], reads=("onesb", "sqsc"), writes=(nb,))
        S.cop("vector", lambda e, nb=nb: e.reduce_max(
            kqmax[:, 0:1], bap(nb)[:, 0:128], AX.X),
            reads=(nb,), writes=("kqmax",), sync_out=True)
        S.cop("vector", lambda e: e.tensor_max(
            kmax2[:, :], kmax2[:, :], kqmax[:, 0:1]),
            reads=("kqmax", "kmax2"), writes=("kmax2",), sync_out=True)

    for t in range(34):
        if 1 <= t < 33:
            ph0_B(t - 1)
        if t < 32:
            ph0_A(t)
        if t >= 2:
            ph0_C(t - 2)

    if _DBG:
        S.dma("scalar", dbg["d_stats"].ap(), dstats[:, :],
              reads=("dstats",))
        S.dma("scalar", dbg["d_ln1o"].ap(), ln1o[:, :], reads=("ln1o",))
        S.dma("scalar", dbg["d_K"].ap(), K[:, :], reads=("K",))
        S.dma("scalar", dbg["d_V"].ap(), V[:, :], reads=("V",))
        S.dma("scalar", dbg["d_kmax2"].ap(), kmax2[:, :], reads=("kmax2",))

    S.cop("scalar", lambda e: e.activation(lnkv[:, :], kmax2[:, :], AF.Ln),
          reads=("kmax2",), writes=("lnkv",), sync_out=True)

    def load_wq(h):
        sl = h % 2
        for hc in range(NHC):
            S.dma("gpsimd", wqb[:, D * sl + 128 * hc:D * sl + 128 * (hc + 1)],
                  wqT[128 * hc:128 * (hc + 1), 128 * h:128 * (h + 1)],
                  grp=("wq", h),
                  writes=((f"wqb{sl}",) if hc == NHC - 1 else ()),
                  war=((f"wqb{sl}",) if hc == 0 else ()))
    load_wq(0)
    load_wq(1)

    S.alias["ytm"] = ["xin0", "xin1", "xsq", "lnt0", "lnt1", "lno0", "lno1",
                      "wkvb"]
    NKT = {0: 8, 1: 16, 2: 24, 3: 32}

    def emit_preamble(h):
        sl = h % 2
        for nch in range(4):
            qb = f"B{nch}"
            S.grp("tensor",
                  [lambda e, hc=hc, sl=sl, nch=nch, qb=qb: e.matmul(
                      bap(qb)[:, :],
                      wqb[:, D * sl + 128 * hc:D * sl + 128 * (hc + 1)],
                      ln1o[:, D * hc + 512 * nch:D * hc + 512 * (nch + 1)],
                      start=(hc == 0), stop=(hc == NHC - 1))
                   for hc in range(NHC)],
                  reads=(f"wqb{sl}", "ln1o"), writes=(qb,))
            S.cop("scalar", lambda e, sl=sl, nch=nch, qb=qb: e.activation(
                Qp[:, D * sl + 512 * nch:D * sl + 512 * (nch + 1)],
                bap(qb)[:, :], AF.Identity, scale=RS2),
                reads=(qb,), writes=(f"Qp{sl}",))
            S.cop("vector", lambda e, sl=sl, nch=nch: e.tensor_tensor(
                sqsc[:, :], Qp[:, D * sl + 512 * nch:D * sl + 512 * (nch + 1)],
                Qp[:, D * sl + 512 * nch:D * sl + 512 * (nch + 1)], ALU.mult),
                reads=(f"Qp{sl}",), writes=("sqsc",))
            nb = "T0" if nch % 2 == 0 else "T1"
            S.grp("tensor", [lambda e, nb=nb: e.matmul(
                bap(nb)[:, :], onesb[:, :], sqsc[:, :], start=True, stop=True)],
                reads=("onesb", "sqsc"), writes=(nb,))
            S.cop("vector", lambda e, nb=nb, nch=nch: e.reduce_max(
                kqmax[:, nch:nch + 1], bap(nb)[:, :], AX.X),
                reads=(nb,), writes=("kqmax",), sync_out=True)
            S.cop("scalar", lambda e, nch=nch: e.activation(
                t0v[:, nch:nch + 1], kqmax[:, nch:nch + 1], AF.Ln),
                reads=("kqmax",), writes=(f"t0-{nch}",), sync_out=True)
            S.cop("vector", lambda e, nch=nch: e.tensor_tensor(
                t0v[:, nch:nch + 1], t0v[:, nch:nch + 1], lnkv[:, :], ALU.add),
                reads=(f"t0-{nch}", "lnkv"), writes=(f"t0-{nch}",),
                sync_out=True)
            S.cop("scalar", lambda e, nch=nch: e.activation(
                negb[:, nch:nch + 1], t0v[:, nch:nch + 1], AF.Exp, scale=0.5),
                reads=(f"t0-{nch}",), writes=(f"negb{nch}",), sync_out=True)
            S.cop("vector", lambda e, nch=nch: e.tensor_scalar(
                negb[:, nch:nch + 1], negb[:, nch:nch + 1], -1.01, 39.5,
                ALU.mult, ALU.add),
                reads=(f"negb{nch}",), writes=(f"negb{nch}",), sync_out=True)
        if h + 2 < H:
            load_wq(h + 2)

    pend_epi = [None]
    emit_preamble(0)
    for h in range(H):
        sl = h % 2
        for qc in range(4):
            nkt = NKT[qc]
            ngrp = nkt // 2

            def slot8(kt):
                return ((kt // 4) % 2) * 4 + kt % 4

            def emit_scores(g, qc=qc, sl=sl):
                for u in range(2):
                    i = 2 * g + u
                    kt = i
                    o = max(128 * (kt // 2) - 512 * qc, 0)
                    bk = f"B{i % 4}"
                    S.grp("tensor", [lambda e, bk=bk, o=o, kt=kt, sl=sl, qc=qc:
                                     e.matmul(bap(bk)[:, o:512],
                                              K[:, 128 * kt:128 * (kt + 1)],
                                              Qp[:, D * sl + 512 * qc + o:
                                                 D * sl + 512 * (qc + 1)],
                                              start=True, stop=True)],
                          reads=("K", f"Qp{sl}"), writes=(bk,))

            def emit_exps(g, qc=qc):
                b0 = (2 * g) % 4
                s0 = slot8(2 * g)
                if g < 4 * qc:
                    S.cop("scalar", lambda e, b0=b0, s0=s0, qc=qc:
                          e.activation(
                              PTb[:, 512 * s0:512 * s0 + 1024],
                              Psc[:, 512 * b0:512 * b0 + 1024], AF.Exp,
                              bias=negb[:, qc:qc + 1]),
                          reads=(f"B{b0}", f"B{b0 + 1}", f"negb{qc}"),
                          writes=(f"PT{s0}", f"PT{s0 + 1}"))
                    return
                for u in range(2):
                    kt = 2 * g + u
                    o = max(128 * (kt // 2) - 512 * qc, 0)
                    s8 = slot8(kt)
                    S.cop("scalar", lambda e, kt=kt, o=o, s8=s8, qc=qc:
                          e.activation(
                              PTb[:, 512 * s8 + o:512 * (s8 + 1)],
                              Psc[:, 512 * (kt % 4) + o:512 * (kt % 4 + 1)],
                              AF.Exp, bias=negb[:, qc:qc + 1]),
                          reads=(f"B{kt % 4}", f"negb{qc}"),
                          writes=(f"PT{s8}",))
                    if (kt // 2) < 4 * qc:
                        continue
                    mt, mres = (tril, "tril") if kt % 2 == 0 else (maskb, "maskb")
                    S.cop("vector", lambda e, s8=s8, o=o, mt=mt:
                          e.tensor_tensor(
                              PTb[:, 512 * s8 + o:512 * s8 + o + 128],
                              PTb[:, 512 * s8 + o:512 * s8 + o + 128],
                              mt[:, :], ALU.mult),
                          reads=(f"PT{s8}", mres), writes=(f"PT{s8}",))

            def emit_avden(g, qc=qc, nkt=nkt):
                fns = []
                rds = set(["V", "onesb"])
                for u in range(2):
                    i = 2 * g + u
                    kt = i
                    o = max(128 * (kt // 2) - 512 * qc, 0)
                    s8 = slot8(kt)
                    rds.add(f"PT{s8}")
                    fns.append(lambda e, kt=kt, s8=s8, i=i, o=o, nkt=nkt:
                               e.matmul(
                                   P["Py"][:, o:512],
                                   V[:, 128 * kt:128 * (kt + 1)],
                                   PTb[:, 512 * s8 + o:512 * (s8 + 1)],
                                   start=(i == 0), stop=(i == nkt - 1)))
                    fns.append(lambda e, s8=s8, i=i, o=o, nkt=nkt: e.matmul(
                        P["Pd"][:, o:512], onesb[:, :],
                        PTb[:, 512 * s8 + o:512 * (s8 + 1)],
                        start=(i == 0), stop=(i == nkt - 1)))
                S.grp("tensor", fns, reads=tuple(rds), writes=("Py", "Pd"))

            prev = None
            for g in range(ngrp):
                emit_scores(g)
                emit_exps(g)
                if g == 0 and pend_epi[0] is not None:
                    pend_epi[0]()
                    pend_epi[0] = None
                if prev is not None:
                    emit_avden(prev)
                if qc == 3 and g == ngrp - 1 and h + 1 < H:
                    emit_preamble(h + 1)
                emit_build(2)
                prev = g
            emit_avden(prev)

            def mk_epi(qc=qc, h=h):
                def go():
                    tb = "T0" if qc % 2 == 0 else "T1"
                    S.cop("scalar", lambda e, qc=qc: e.copy(
                        yfm[:, 512 * qc:512 * (qc + 1)], P["Py"][:, :]),
                        reads=("Py",), writes=("yfm",))
                    S.cop("vector", lambda e: e.tensor_copy(
                        denrb[:, :], P["Pd"][:, :]),
                        reads=("Pd",), writes=("denrb",), sync_out=True)
                    fns = [lambda e, jq=jq, qc=qc, tb=tb: e.transpose(
                        tbank_bf16(tb)[:, 128 * jq:128 * (jq + 1)],
                        yfm[:, 512 * qc + 128 * jq:512 * qc + 128 * (jq + 1)],
                        ident[:, :]) for jq in range(4)]
                    fns += [lambda e, jq=jq, tb=tb: e.transpose(
                        tbank_bf16(tb)[:, 512 + 128 * jq:512 + 128 * (jq + 1)],
                        denrb[:, 128 * jq:128 * (jq + 1)],
                        ident[:, :]) for jq in range(4)]
                    S.grp("tensor", fns, reads=("yfm", "denrb", "ident"),
                          writes=(tb,))
                    S.cop("vector", lambda e, qc=qc, tb=tb: e.tensor_copy(
                        denTr[:, 4 * qc:4 * qc + 4],
                        tbank_bf16(tb)[:, 512:1024].rearrange(
                            "p (j i) -> p j i", i=128)[:, :, 0:1]),
                        reads=(tb,), writes=("denTr",), sync_out=True)
                    S.cop("vector", lambda e, qc=qc: e.reciprocal(
                        denT[:, 4 * qc:4 * qc + 4], denTr[:, 4 * qc:4 * qc + 4]),
                        reads=("denTr",), writes=("denT",), sync_out=True)
                    for jq in range(4):
                        qt = 4 * qc + jq
                        S.cop("vector", lambda e, jq=jq, qt=qt, h=h, tb=tb:
                              e.tensor_scalar(
                                  ytm[:, D * qt + 128 * h:
                                      D * qt + 128 * (h + 1)],
                                  tbank_bf16(tb)[:, 128 * jq:128 * (jq + 1)],
                                  denT[:, qt:qt + 1], None, ALU.mult),
                              reads=(tb, "denT"), writes=("ytm",))
                return go
            pend_epi[0] = mk_epi()
    if pend_epi[0] is not None:
        pend_epi[0]()
        pend_epi[0] = None
        if _DBG and h == 0:
            S.dma("scalar", dbg["d_qp0"].ap(), Qp[:, 0:D], reads=("Qp0",))
            S.dma("scalar", dbg["d_wqb0"].ap(), wqb[:, 0:D],
                  reads=("wqb0",))
            S.dma("scalar", dbg["d_negb0"].ap(), negb[:, :],
                  reads=tuple(f"negb{i}" for i in range(4)))
            S.dma("scalar", dbg["d_denT0"].ap(), denT[:, :], reads=("denT",))
            S.dma("scalar", dbg["d_yfm0"].ap(), yfm[:, :], reads=("yfm",))

    if _DBG:
        S.dma("scalar", dbg["d_ytm"].ap(), ytm[:, :], reads=("ytm",))

    for r in ("xin2-0", "xin2-1", "x2t0", "x2t1", "ln2tm0", "ln2tm1",
              "ln2sb0", "ln2sb1", "xsq2"):
        S.alias[r] = ["ln1o"]

    for r in ("urhs", "urhsB", "ln2ts", "x2r0", "x2r1", "x2r2", "x2r3",
              "wdrsA0", "wdrsA1", "wdrsB0", "wdrsB1"):
        S.alias[r] = ["ln1o", "xin2-0", "xin2-1", "x2t0", "x2t1",
                      "ln2tm0", "ln2tm1", "ln2sb0", "ln2sb1", "xsq2"]
    S.alias["hact"] = ["ytm"]
    S.alias["grhs"] = ["ytm"]
    S.alias["grhsB"] = ["ytm"]
    NI5 = 11

    done_ld = set()

    def emit_ln2ts(rt):
        if rt > 3 or ("ln2ts", rt) in done_ld:
            return
        done_ld.add(("ln2ts", rt))
        for hc in range(NHC):
            srcap = bass.AP(ln2_scr, 4 * rt * (128 * 2048) + 128 * hc,
                            [[2048, 128], [128 * 2048, 4], [1, 128]])
            S.dma("sync", ln2ts[:, 512 * hc:512 * (hc + 1)],
                  srcap, grp=("ln2ts", rt),
                  reads=tuple(f"ln2s{4 * rt + s}" for s in range(4)),
                  war=(("ln2ts",) if hc == 0 else ()),
                  writes=(("ln2ts",) if hc == NHC - 1 else ()))

    def emit_gfill(rt, i5):
        if rt > 3 or ("g", rt, i5) in done_ld:
            return
        done_ld.add(("g", rt, i5))
        iw = 512 if i5 < NI5 - 1 else ITER - 512 * (NI5 - 1)
        for hc in range(NHC):
            half = "grhs" if hc % 2 == 0 else "grhsB"
            S.dma("sync" if hc % 2 == 0 else "gpsimd",
                  grhs[:, 512 * hc:512 * hc + iw],
                  g_full[128 * hc:128 * (hc + 1), 512 * i5:512 * i5 + iw],
                  grp=("g", rt, i5, hc % 2), reads=("g_full",),
                  war=((half,) if hc < 2 else ()),
                  writes=((half,) if hc >= NHC - 2 else ()))

    def emit_ufill(rt, i5):
        if rt > 3 or ("u", rt, i5) in done_ld:
            return
        done_ld.add(("u", rt, i5))
        iw = 512 if i5 < NI5 - 1 else ITER - 512 * (NI5 - 1)
        for hc in range(NHC):
            half = "urhs" if hc % 2 == 0 else "urhsB"
            S.dma("sync" if hc % 2 == 0 else "gpsimd",
                  urhs[:, 512 * hc:512 * hc + iw],
                  u_full[128 * hc:128 * (hc + 1), 512 * i5:512 * i5 + iw],
                  grp=("u", rt, i5, hc % 2), reads=("u_full",),
                  war=((half,) if hc < 2 else ()),
                  writes=((half,) if hc >= NHC - 2 else ()))

    def ph2c_A(st):
        sl = st % 2
        S.dma("sync", xin2[:, D * sl:D * (sl + 1)],
              xq[128 * st:128 * (st + 1), :], writes=(f"xin2-{sl}",))
        S.cop("vector", lambda e, sl=sl, st=st: e.tensor_tensor(
            x2t[:, D * sl:D * (sl + 1)], xin2[:, D * sl:D * (sl + 1)],
            ytm[:, D * st:D * (st + 1)], ALU.add),
            reads=(f"xin2-{sl}", "ytm"), writes=(f"x2t{sl}",))
        S.dma("gpsimd", x2_scr[st:st + 1, :].rearrange("o (p d) -> (o p) d",
                                                       p=128),
              x2t[:, D * sl:D * (sl + 1)], grp="x2scr",
              reads=(f"x2t{sl}",), writes=("x2_scr",))
        ln_front(f"x2t{sl}",
                 lambda sl=sl: x2t[:, D * sl:D * (sl + 1)], sl,
                 sq=xsq2, sq_res="xsq2")

    def ph2c_B(st):
        sl = st % 2
        ln_back(f"x2t{sl}",
                lambda sl=sl: x2t[:, D * sl:D * (sl + 1)],
                ln2tm[:, D * sl:D * (sl + 1)], sl, f"ln2tm{sl}")

    def ph2c_C(st):
        sl = st % 2

        def dst_fn(g, sl=sl):
            col0 = NHC * 128 * sl + 128 * (4 * g)
            ap = bass.AP(ln2sb, col0, [[2 * NHC * 128, 128], [128, 4], [1, 128]])
            return ap, f"ln2sb{sl}"
        tp16(f"ln2tm{sl}",
             lambda hc, sl=sl: ln2tm[:, D * sl + 128 * hc:D * sl + 128 * (hc + 1)],
             dst_fn, f"ln2tp{st}")
        S.dma("gpsimd",
              ln2_scr[st:st + 1, :].rearrange("o (p d) -> (o p) d", p=128),
              ln2sb[:, NHC * 128 * sl:NHC * 128 * (sl + 1)], grp="ln2scr",
              reads=(f"ln2sb{sl}",), writes=(f"ln2s{st}",))

    for t in range(18):
        if 1 <= t < 17:
            ph2c_B(t - 1)
        if t < 16:
            ph2c_A(t)
        if t == 16:
            emit_build(len(tiles) + 1)
            for csem in ("C0", "C1", "C2"):
                S._wait("sync", csem, 1)
                S._wait("gpsimd", csem, 1)
            emit_gfill(0, 0)
        if t >= 2:
            ph2c_C(t - 2)

    BS4 = (("B0", "B1", "B2", "B3"), ("Py", "Pd", "T0", "T1"))
    for rt in range(4):
        emit_ln2ts(rt)
        for i5 in range(NI5):
            iw = 512 if i5 < NI5 - 1 else ITER - 512 * (NI5 - 1)
            nib = iw // 128
            emit_gfill(rt, i5)
            for ib in range(nib):
                gb = f"B{ib}"
                ic = 4 * i5 + ib
                S.grp("tensor",
                      [lambda e, hc=hc, ib=ib, gb=gb: e.matmul(
                          bap(gb)[:, :],
                          grhs[:, 512 * hc + 128 * ib:512 * hc + 128 * (ib + 1)],
                          ln2ts[:, 512 * hc:512 * (hc + 1)],
                          start=(hc == 0), stop=(hc == NHC - 1))
                       for hc in range(NHC)],
                      reads=("ln2ts", "grhs", "grhsB"), writes=(gb,))
                S.cop("scalar", lambda e, ic=ic, gb=gb:
                      e.activation(hact[:, 512 * ic:512 * (ic + 1)],
                                   bap(gb)[:, :], AF.Silu),
                      reads=(gb,), writes=("hact",))
            emit_ufill(rt, i5)
            for ib in range(nib):
                ub = f"B{ib}"
                ic = 4 * i5 + ib
                S.grp("tensor",
                      [lambda e, hc=hc, ib=ib, ub=ub: e.matmul(
                          bap(ub)[:, :],
                          urhs[:, 512 * hc + 128 * ib:512 * hc + 128 * (ib + 1)],
                          ln2ts[:, 512 * hc:512 * (hc + 1)],
                          start=(hc == 0), stop=(hc == NHC - 1))
                       for hc in range(NHC)],
                      reads=("ln2ts", "urhs", "urhsB"), writes=(ub,))
                S.cop("vector", lambda e, ic=ic, ub=ub:
                      e.tensor_tensor(
                          hact[:, 512 * ic:512 * (ic + 1)],
                          hact[:, 512 * ic:512 * (ic + 1)],
                          bap(ub)[:, :], ALU.mult),
                      reads=(ub, "hact"), writes=("hact",))
        if _DBG and rt == 0:
            S.dma("scalar", dbg["d_hact0"].ap(), hact[:, :],
                  reads=("hact",))
        emit_ln2ts(rt + 1)
        emit_gfill(rt + 1, 0)
        emit_ufill(rt + 1, 0)
        for ph in range(2):
            for rb in range(4):
                st = rt * 4 + rb
                S.dma("sync",
                      x2rl[:, 1024 * rb:1024 * (rb + 1)],
                      bass.AP(x2_scr, st * (128 * D) + 1024 * ph,
                              [[D, 128], [1, 1024]]),
                      grp="x2r",
                      reads=("x2_scr",), writes=(f"x2r{rb}",))
            for ic in range(NIC):
                wsl = ic % 2
                wres = (f"wdrsA{wsl}", f"wdrsB{wsl}")
                S.dma("sync", wdrs[:, 1024 * wsl:1024 * wsl + 512],
                      d_full[128 * ic:128 * (ic + 1),
                             1024 * ph:1024 * ph + 512],
                      grp=("wd", wsl, 0),
                      reads=("d_full",), writes=(wres[0],))
                S.dma("gpsimd", wdrs[:, 1024 * wsl + 512:1024 * (wsl + 1)],
                      d_full[128 * ic:128 * (ic + 1),
                             1024 * ph + 512:1024 * (ph + 1)],
                      grp=("wd", wsl, 1),
                      reads=("d_full",), writes=(wres[1],))
                fns = []
                for rb in range(4):
                    for hh in range(2):
                        fns.append(lambda e, rb=rb, ic=ic, hh=hh, wsl=wsl,
                                   bk=BANKS8[2 * rb + hh]: e.matmul(
                                       bap(bk)[:, :],
                                       hact[:, 512 * ic + 128 * rb:
                                            512 * ic + 128 * (rb + 1)],
                                       wdrs[:, 1024 * wsl + 512 * hh:
                                            1024 * wsl + 512 * (hh + 1)],
                                       start=(ic == 0), stop=(ic == NIC - 1)))
                S.grp("tensor", fns,
                      reads=("hact",) + wres,
                      writes=tuple(BANKS8))
            for rb in range(4):
                st = rt * 4 + rb
                for hh in range(2):
                    bk = BANKS8[2 * rb + hh]
                    S.cop("vector", lambda e, rb=rb, hh=hh, bk=bk:
                          e.tensor_tensor(
                              x2rl[:, 1024 * rb + 512 * hh:
                                   1024 * rb + 512 * (hh + 1)],
                              x2rl[:, 1024 * rb + 512 * hh:
                                   1024 * rb + 512 * (hh + 1)],
                              bap(bk)[:, :], ALU.add),
                          reads=(bk, f"x2r{rb}"),
                          writes=(f"x2r{rb}",))
                S.dma("scalar",
                      y_out[128 * st:128 * (st + 1),
                            1024 * ph:1024 * (ph + 1)],
                      x2rl[:, 1024 * rb:1024 * (rb + 1)],
                      grp="yout",
                      reads=(f"x2r{rb}",))

    if _DBG:
        S.dma("gpsimd", dbg["d_x2"].ap(), x2_scr.ap(), reads=("x2_scr",))
        S.dma("gpsimd", dbg["d_ln2"].ap(), ln2_scr.ap(),
              reads=("ln2_scr",))
        S.dma("gpsimd", dbg["d_gfull"].ap(), g_full[0:128, :],
              reads=("g_full",))
        S.dma("gpsimd", dbg["d_dfull"].ap(), d_full[0:128, :],
              reads=("d_full",))

    S.drain_dmas("scalar", queues=("scalar",))
    S.drain_dmas("sync", queues=("sync",))
    S.drain_dmas("gpsimd", queues=("gpsimd",))
    S.verify()

    with nc.Block() as block:
        @block.sync
        def _(e):
            for item in S.q["sync"]:
                item[3](e)

        @block.scalar
        def _(e):
            for item in S.q["scalar"]:
                item[3](e)

        @block.vector
        def _(e):
            for item in S.q["vector"]:
                item[3](e)

        @block.tensor
        def _(e):
            for item in S.q["tensor"]:
                item[3](e)

        @block.gpsimd
        def _(e):
            for item in S.q["gpsimd"]:
                item[3](e)

    nc.reset()
    nc.finalize()
    return nc


_NC_CACHE = {}


def kernel(L0_attn_x, L0_attn_w_kv, L0_attn_w_q, L0_ffn_lut, L0_ffn_gate_idx,
           L0_ffn_up_idx, L0_ffn_down_idx):
    x = np.asarray(L0_attn_x, dtype=np.float32)
    w_kv = np.asarray(L0_attn_w_kv, dtype=np.float32)
    w_q = np.asarray(L0_attn_w_q, dtype=np.float32)
    lut = np.asarray(L0_ffn_lut, dtype=np.float32)
    g_i = np.asarray(L0_ffn_gate_idx, dtype=np.int32)
    u_i = np.asarray(L0_ffn_up_idx, dtype=np.int32)
    d_i = np.asarray(L0_ffn_down_idx, dtype=np.int32)

    if "nc" not in _NC_CACHE:
        _NC_CACHE["nc"] = build_nc()
    nc = _NC_CACHE["nc"]

    wqT_h = np.ascontiguousarray(w_q.T)
    wkvT_h = np.ascontiguousarray(w_kv.T)
    lutb_h = np.ascontiguousarray(np.broadcast_to(lut[None, :], (128, 16)))
    gT = g_i.T
    uT = u_i.T
    dT = d_i.T

    in_maps = make_in_maps(x, wqT_h, wkvT_h, lutb_h, gT, uT, dT)
    res = run_bass_kernel_spmd(nc, in_maps, core_ids=list(range(8)))
    out = np.empty((4, T, D), dtype=np.float32)
    for c in range(8):
        b, half = c // 2, c % 2
        yo = np.asarray(res.results[c]["y_out"]).reshape(16, 128, D)
        for j in range(16):
            r0 = 128 * (2 * j + half)
            out[b, r0:r0 + 128, :] = yo[j]
    return out


def make_in_maps(x, wqT_h, wkvT_h, lutb_h, gT, uT, dT):
    in_maps = []
    for c in range(8):
        b, half = c // 2, c % 2
        xb = x[b].reshape(32, 128, D)
        own = np.ascontiguousarray(
            xb[[2 * j + half for j in range(16)]]).reshape(2048, D)
        oth = np.ascontiguousarray(
            xb[[2 * j + 1 - half for j in range(16)]]).reshape(2048, D)
        mb = np.full((128, 128), float(half), dtype=ml_dtypes.bfloat16)
        in_maps.append({
            "xq": own, "xo": oth, "wqT": wqT_h, "wkvT": wkvT_h, "lutb": lutb_h,
            "gidx": np.ascontiguousarray(gT[256 * c:256 * (c + 1)]),
            "uidx": np.ascontiguousarray(uT[256 * c:256 * (c + 1)]),
            "didx": np.ascontiguousarray(dT[688 * c:688 * (c + 1)]),
            "maskb": mb,
        })
    return in_maps


# version 29 (final)
# speedup vs baseline: 1.0237x; 1.0237x over previous
import math
import os
import numpy as np
import ml_dtypes
import concourse.bass as bass
import concourse.mybir as mybir
from concourse.bass_utils import run_bass_kernel_spmd

F32 = mybir.dt.float32
BF16 = mybir.dt.bfloat16
I32 = mybir.dt.int32
AF = mybir.ActivationFunctionType
ALU = mybir.AluOpType
AX = mybir.AxisListType

_DBG = bool(int(os.environ.get("MK_DBG", "0")))
D = 2048
T = 4096
H = 16
ITER = 5504
NIC = ITER // 128
NHC = D // 128
RS2 = 1.0 / math.sqrt(128.0)
LN_EPS = 1e-5

ENGINES = ("sync", "scalar", "vector", "tensor", "gpsimd")
ENG_SEM = {"tensor": "PE", "scalar": "AC", "vector": "DV", "gpsimd": "PL"}
COMPUTE_SEMS = {"PE", "AC", "DV", "PL"}


class Sched:

    FORCE_SYNC_OUT = bool(int(os.environ.get("MK_SYNCALL", "1")))

    DQ = {"sync": ("LD", 16), "scalar": ("LS", 4), "gpsimd": ("LG", 4)}

    def __init__(self, sems):
        self.sems = sems
        self.q = {e: [] for e in ENGINES}
        self.cnt = {s: 0 for s in sems}
        self.res = {}
        self.hw = {}
        self.alias = {}
        self.dma_rot = {e: 0 for e in self.DQ}
        self.dma_grp = {}

    def _wait(self, eng, sem, val):
        if val <= 0:
            return
        key = (eng, sem)
        if self.hw.get(key, 0) >= val:
            return
        self.hw[key] = val
        s = self.sems[sem]
        self.q[eng].append(("w", sem, val,
                            lambda e, s=s, v=val: e.wait_ge(s, v)))

    def raw(self, eng, fn):
        self.q[eng].append(("o", None, None, fn))

    def _entry_wait(self, eng, entry, is_dma, war=False):
        sem, val, peng = entry
        if (not is_dma) and (not war) and peng == eng and sem in COMPUTE_SEMS:
            return
        if is_dma and peng == eng and sem not in COMPUTE_SEMS:
            return
        self._wait(eng, sem, val)

    def _pre(self, eng, reads, writes, is_dma):
        for r in writes:
            for stale in self.alias.pop(r, ()):
                st = self.res.get(stale)
                if st:
                    if st["w"]:
                        self._entry_wait(eng, st["w"], is_dma)
                    for e2 in st["r"].values():
                        self._entry_wait(eng, e2, is_dma, war=True)
        for r in reads:
            st = self.res.get(r)
            if st and st["w"]:
                self._entry_wait(eng, st["w"], is_dma)
        for r in writes:
            st = self.res.get(r)
            if st:
                if st["w"]:
                    self._entry_wait(eng, st["w"], is_dma)
                for e2 in st["r"].values():
                    self._entry_wait(eng, e2, is_dma, war=True)

    def _post(self, eng, reads, writes, sem, val, sync_out=False):
        if self.FORCE_SYNC_OUT:
            sync_out = True
        for r in writes:
            self.res[r] = {"w": (sem, val, None if sync_out else eng), "r": {}}
        for r in reads:
            st = self.res.setdefault(r, {"w": None, "r": {}})
            k = (sem, eng)
            if st["r"].get(k, (None, 0, None))[1] < val:
                st["r"][k] = (sem, val, eng)

    def cop(self, eng, fn, reads=(), writes=(), sem=None, k=1,
            sync_out=False):
        sem = sem or ENG_SEM[eng]
        self._pre(eng, reads, writes, False)
        self.cnt[sem] += k
        val = self.cnt[sem]
        s = self.sems[sem]
        self.q[eng].append(("i", sem, k,
                            lambda e, fn=fn, s=s, k=k: fn(e).then_inc(s, k)))
        self._post(eng, reads, writes, sem, val, sync_out)
        return val

    def grp(self, eng, fns, reads=(), writes=()):
        sem = ENG_SEM[eng]
        self._pre(eng, reads, writes, False)
        for fn in fns[:-1]:
            self.raw(eng, fn)
        self.cnt[sem] += 1
        val = self.cnt[sem]
        s = self.sems[sem]
        self.q[eng].append(("i", sem, 1,
                            lambda e, fn=fns[-1], s=s: fn(e).then_inc(s, 1)))
        self._post(eng, reads, writes, sem, val)
        return val

    def dma(self, eng, out, in_, sem=None, reads=(), writes=(), war=(),
            grp=None):
        base, M = self.DQ[eng]
        if grp is None or grp not in self.dma_grp:
            i = self.dma_rot[eng] % M
            self.dma_rot[eng] += 1
            psem = f"{base}{i}"
            if grp is not None:
                self.dma_grp[grp] = psem
            self._wait(eng, psem, self.cnt[psem])
        else:
            psem = self.dma_grp[grp]
        self._pre(eng, reads, tuple(writes) + tuple(war), True)
        self.cnt[psem] += 16
        val = self.cnt[psem]
        s = self.sems[psem]
        self.q[eng].append(("i", psem, 16,
                            lambda e, out=out, in_=in_, s=s:
                            e.dma_start(out=out, in_=in_).then_inc(s, 16)))
        self._post(eng, reads, writes, psem, val)
        return psem, val

    def drain_dmas(self, eng_wait="scalar", queues=("sync", "scalar",
                                                    "gpsimd")):
        for qeng in queues:
            base, M = self.DQ[qeng]
            for i in range(M):
                psem = f"{base}{i}"
                self._wait(eng_wait, psem, self.cnt[psem])

    def verify(self):
        pos = {e: 0 for e in ENGINES}
        vals = {s: 0 for s in self.sems}
        progress = True
        while progress:
            progress = False
            for e in ENGINES:
                while pos[e] < len(self.q[e]):
                    kind, sem, v, _ = self.q[e][pos[e]]
                    if kind == "w" and vals[sem] < v:
                        break
                    if kind == "i":
                        vals[sem] += v
                    pos[e] += 1
                    progress = True
        stuck = {e: (pos[e], len(self.q[e])) for e in ENGINES
                 if pos[e] < len(self.q[e])}
        if stuck:
            detail = {}
            for e in stuck:
                kind, sem, v, _ = self.q[e][pos[e]]
                detail[e] = (pos[e], kind, sem, v,
                             vals.get(sem) if sem else None)
            raise RuntimeError(f"schedule deadlock: {detail}")


def build_nc():
    nc = bass.Bass("TRN2", target_bir_lowering=False)

    xq = nc.dram_tensor("xq", [2048, D], F32, kind="ExternalInput")
    xo = nc.dram_tensor("xo", [2048, D], F32, kind="ExternalInput")
    wqT = nc.dram_tensor("wqT", [D, D], F32, kind="ExternalInput")
    wkvT = nc.dram_tensor("wkvT", [D, 256], F32, kind="ExternalInput")
    lutb = nc.dram_tensor("lutb", [128, 16], F32, kind="ExternalInput")
    gidx = nc.dram_tensor("gidx", [256, ITER], I32, kind="ExternalInput")
    uidx = nc.dram_tensor("uidx", [256, ITER], I32, kind="ExternalInput")
    didx = nc.dram_tensor("didx", [688, D], I32, kind="ExternalInput")
    maskb_d = nc.dram_tensor("maskb", [128, 128], BF16, kind="ExternalInput")
    y_out = nc.dram_tensor("y_out", [2048, D], F32, kind="ExternalOutput")

    g_shard = nc.dram_tensor("g_shard", [256, ITER], BF16, kind="Internal")
    u_shard = nc.dram_tensor("u_shard", [256, ITER], BF16, kind="Internal")
    d_shard = nc.dram_tensor("d_shard", [688, D], BF16, kind="Internal")
    g_full = nc.dram_tensor("g_full", [D, ITER], BF16, kind="Internal", addr_space="Shared")
    u_full = nc.dram_tensor("u_full", [D, ITER], BF16, kind="Internal", addr_space="Shared")
    d_full = nc.dram_tensor("d_full", [ITER, D], BF16, kind="Internal", addr_space="Shared")
    x2_scr = nc.dram_tensor("x2_scr", [16, 128 * D], F32, kind="Internal")
    ln2_scr = nc.dram_tensor("ln2_scr", [16, 128 * 2048], BF16, kind="Internal")
    dbg = {}
    dstats = None
    if _DBG:
        dstats = nc.alloc_sbuf_tensor("dstats", [128, 3 * 48], F32)
        for nm, shape, dty in (("d_ln1o", [128, NHC * D], BF16),
                               ("d_K", [128, T], BF16),
                               ("d_V", [128, T], BF16),
                               ("d_kmax2", [128, 1], F32),
                               ("d_qp0", [128, D], BF16),
                               ("d_wqb0", [128, D], BF16),
                               ("d_negb0", [128, 4], F32),
                               ("d_denT0", [128, 16], F32),
                               ("d_yfm0", [128, D], BF16),
                               ("d_ytm", [128, 16 * D], BF16),
                               ("d_x2", [16, 128 * D], F32),
                               ("d_ln2", [16, 128 * 2048], BF16),
                               ("d_gfull", [128, ITER], BF16),
                               ("d_dfull", [128, D], BF16),
                               ("d_hact0", [128, 4 * ITER], BF16),
                               ("d_stats", [128, 3 * 48], F32)):
            dbg[nm] = nc.dram_tensor(nm, shape, dty, kind="ExternalOutput")

    tri = np.zeros((128, 128), dtype=np.float32)
    tri[np.triu_indices(128)] = 1.0
    ident_b = nc.inline_tensor(np.eye(128, dtype=ml_dtypes.bfloat16), name="c_ib")
    ident_f = nc.inline_tensor(np.eye(128, dtype=np.float32), name="c_if")
    tril_b = nc.inline_tensor(tri.astype(ml_dtypes.bfloat16), name="c_tri")
    ones_b = nc.inline_tensor(np.ones((128, 128), dtype=ml_dtypes.bfloat16),
                              name="c_on")

    def sb(name, shape, dtype):
        return nc.alloc_sbuf_tensor(name, shape, dtype)

    ident = sb("ident", [128, 128], BF16)
    identf = sb("identf", [128, 128], F32)
    tril = sb("tril", [128, 128], BF16)
    maskb = sb("maskb_s", [128, 128], BF16)
    onesb = sb("onesb_s", [128, 128], BF16)
    lutbs = sb("lutbs", [128, 16], F32)
    K = sb("K", [128, T], BF16)
    V = sb("V", [128, T], BF16)

    Qp = sb("Qp", [128, 2 * D], BF16)
    wqb = sb("wqb", [128, 2 * D], BF16)
    sqsc = sb("sqsc", [128, 512], BF16)
    PTb = sb("PTb", [128, 8 * 512], BF16)
    denrb = sb("denrb", [128, 512], BF16)
    denTr = sb("denTr", [128, 16], F32)
    denT = sb("denT", [128, 16], F32)
    yfm = sb("yfm", [128, D], BF16)
    kqmax = sb("kqmax", [128, 8], F32)
    kmax2 = sb("kmax2", [128, 1], F32)
    qmax2 = sb("qmax2", [128, 1], F32)
    negb = sb("negb", [128, 4], F32)
    t0v = sb("t0v", [128, 4], F32)
    lnkv = sb("lnkv", [128, 1], F32)
    s1v = sb("s1v", [128, 2], F32)
    s2v = sb("s2v", [128, 2], F32)
    mvec = sb("mvec", [128, 2], F32)
    vvar = sb("vvar", [128, 2], F32)
    varl = sb("varl", [128, 2], F32)
    rstd = sb("rstd", [128, 2], F32)
    nmr = sb("nmr", [128, 2], F32)
    msqn = sb("msqn", [128, 1], F32)
    epsv = sb("epsv", [128, 1], F32)
    upsc = sb("upsc", [128, 512], BF16)
    bidx = sb("bidx", [128, 2 * 512], I32)
    bidxb = sb("bidxb", [128, 512], BF16)
    bt = sb("bt", [128, 512], BF16)
    bacc = sb("bacc", [128, 2 * 512], BF16)
    bacc2 = sb("bacc2", [128, 2 * 512], BF16)
    bt2 = sb("bt2", [128, 512], BF16)

    base = (nc.sbuf_base + 31) & ~31
    R1 = base
    R2 = R1 + 65536
    assert R2 + 65536 <= 229344 - 64, f"SBUF overflow {R2 + 65536}"

    def sba(name, shape, dtype, off):
        return nc.alloc_sbuf_tensor_at(name, shape, dtype, offset=off)

    ln1o = sba("ln1o", [128, NHC * D], BF16, R1)
    xin2 = sba("xin2", [128, 2 * D], F32, R1)
    x2t = sba("x2t", [128, 2 * D], F32, R1 + 16384)
    ln2tm = sba("ln2tm", [128, 2 * D], BF16, R1 + 32768)
    ln2sb = sba("ln2sb", [128, 2 * NHC * 128], BF16, R1 + 40960)
    xsq2 = sba("xsq2", [128, D], BF16, R1 + 49152)
    urhs = sba("urhs", [128, NHC * 512], BF16, R1)
    ln2ts = sba("ln2ts", [128, NHC * 512], BF16, R1 + 16384)
    x2rl = sba("x2rl", [128, 2 * D], F32, R1 + 32768)
    wdrs = sba("wdrs", [128, 2 * D], BF16, R1 + 49152)
    xin = sba("xin", [128, 2 * D], F32, R2)
    xsq = sba("xsq", [128, D], BF16, R2 + 16384)
    lnt = sba("lnt", [128, 2 * D], BF16, R2 + 20480)
    lno = sba("lno", [128, NHC * 256], BF16, R2 + 28672)
    wkvb = sba("wkvb", [128, NHC * 256], BF16, R2 + 36864)
    xin3 = sba("xin3", [128, D], F32, R2 + 45056)
    ytm = sba("ytm", [128, 16 * D], BF16, R2)
    hact = sba("hact", [128, 4 * ITER], BF16, R2)
    grhs = sba("grhs", [128, NHC * 512], BF16, R2 + 44032)

    Psc = nc.alloc_psum_tensor("Psc", [128, 4 * 512], F32)
    P = {}
    for name in ("Py", "Pd", "T0", "T1"):
        P[name] = nc.alloc_psum_tensor(name, [128, 512], F32)
    BANKS8 = ("B0", "B1", "B2", "B3", "Py", "Pd", "T0", "T1")

    def bap(name):
        if name.startswith("B"):
            i = int(name[1])
            return Psc[:, 512 * i:512 * (i + 1)]
        return P[name][:, :]

    def tbank_bf16(name):
        return P[name].ap().bitcast(BF16)

    sems = {}
    names = ["PE", "AC", "DV", "PL", "C0", "C1", "C2"]
    names += [f"LD{i}" for i in range(16)]
    names += [f"LS{i}" for i in range(4)]
    names += [f"LG{i}" for i in range(4)]
    for s in names:
        sems[s] = nc.alloc_semaphore(name=s)
    S = Sched(sems)

    for dst, src, rs in ((ident, ident_b, "ident"), (identf, ident_f, "identf"),
                         (tril, tril_b, "tril"), (maskb, maskb_d, "maskb"),
                         (onesb, ones_b, "onesb"), (lutbs, lutb, "lutbs")):
        S.dma("sync", dst[:, :], src.ap(), writes=(rs,))
    for hc in range(NHC):
        S.dma("gpsimd", wkvb[:, 256 * hc:256 * (hc + 1)],
              wkvT[128 * hc:128 * (hc + 1), :], grp="wkv",
              writes=("wkvb",) if hc == NHC - 1 else ())

    tiles = []
    for (src, dst, nrows, ncols) in ((gidx, g_shard, 256, ITER),
                                     (uidx, u_shard, 256, ITER),
                                     (didx, d_shard, 688, D)):
        p0 = 0
        while p0 < nrows:
            pr = min(128, nrows - p0)
            c0 = 0
            while c0 < ncols:
                cw = min(512, ncols - c0)
                tiles.append((src, dst, p0, pr, c0, cw))
                c0 += cw
            p0 += pr

    NPOOL_L = 0

    def emit_build_tile_steps(i):
        src_, dst, p0, pr, c0, cw = tiles[i]
        sl = i % 2
        S.dma("sync", bidx[0:pr, 512 * sl:512 * sl + cw],
              src_[p0:p0 + pr, c0:c0 + cw], writes=(f"bidx{sl}",))
        S.cop("vector", lambda e, pr=pr, cw=cw, sl=sl: e.tensor_copy(
            bidxb[0:pr, 0:cw], bidx[0:pr, 512 * sl:512 * sl + cw]),
            reads=(f"bidx{sl}", "lutbs"), writes=("bidxb",))
        yield
        for l in range(16 - NPOOL_L):
            if l == 0:
                S.cop("vector", lambda e, pr=pr, cw=cw, sl=sl, l=l:
                      e.tensor_scalar(bacc[0:pr, 512 * sl:512 * sl + cw],
                                      bidxb[0:pr, 0:cw], float(l),
                                      lutbs[0:pr, l:l + 1],
                                      ALU.is_equal, ALU.mult),
                      reads=("bidxb",), writes=(f"bacc{sl}",))
            else:
                S.cop("vector", lambda e, pr=pr, cw=cw, l=l:
                      e.tensor_scalar(bt[0:pr, 0:cw], bidxb[0:pr, 0:cw],
                                      float(l), lutbs[0:pr, l:l + 1],
                                      ALU.is_equal, ALU.mult),
                      reads=("bidxb",), writes=("bt",))
                S.cop("vector", lambda e, pr=pr, cw=cw, sl=sl:
                      e.tensor_tensor(bacc[0:pr, 512 * sl:512 * sl + cw],
                                      bacc[0:pr, 512 * sl:512 * sl + cw],
                                      bt[0:pr, 0:cw], ALU.add),
                      reads=("bt", f"bacc{sl}"), writes=(f"bacc{sl}",))
            if l % 2 == 1:
                yield
        for l in range(16 - NPOOL_L, 16) if NPOOL_L else ():
            if l == 16 - NPOOL_L:
                S.cop("gpsimd", lambda e, pr=pr, cw=cw, sl=sl, l=l:
                      e.tensor_scalar(bacc2[0:pr, 512 * sl:512 * sl + cw],
                                      bidxb[0:pr, 0:cw], float(l),
                                      lutbs[0:pr, l:l + 1],
                                      ALU.is_equal, ALU.mult),
                      reads=("bidxb",), writes=(f"bacc2{sl}",))
            else:
                S.cop("gpsimd", lambda e, pr=pr, cw=cw, l=l:
                      e.tensor_scalar(bt2[0:pr, 0:cw], bidxb[0:pr, 0:cw],
                                      float(l), lutbs[0:pr, l:l + 1],
                                      ALU.is_equal, ALU.mult),
                      reads=("bidxb",), writes=("bt2",))
                S.cop("gpsimd", lambda e, pr=pr, cw=cw, sl=sl:
                      e.tensor_tensor(bacc2[0:pr, 512 * sl:512 * sl + cw],
                                      bacc2[0:pr, 512 * sl:512 * sl + cw],
                                      bt2[0:pr, 0:cw], ALU.add),
                      reads=("bt2", f"bacc2{sl}"), writes=(f"bacc2{sl}",))
            yield
        if NPOOL_L:
            S.cop("vector", lambda e, pr=pr, cw=cw, sl=sl:
                  e.tensor_tensor(bacc[0:pr, 512 * sl:512 * sl + cw],
                                  bacc[0:pr, 512 * sl:512 * sl + cw],
                                  bacc2[0:pr, 512 * sl:512 * sl + cw],
                                  ALU.add),
                  reads=(f"bacc2{sl}", f"bacc{sl}"), writes=(f"bacc{sl}",))
        S.dma("gpsimd", dst[p0:p0 + pr, c0:c0 + cw],
              bacc[0:pr, 512 * sl:512 * sl + cw], grp=("shard", dst.name),
              reads=(f"bacc{sl}",), writes=(dst.name,))
        yield

    def _build_gen():
        for i in range(len(tiles)):
            for step in emit_build_tile_steps(i):
                yield
        for sh, fl, csem in ((g_shard, g_full, "C0"),
                             (u_shard, u_full, "C1"),
                             (d_shard, d_full, "C2")):
            S.cop("gpsimd", lambda e, sh=sh, fl=fl: e.collective_compute(
                "AllGather", ALU.bypass, replica_groups=[list(range(8))],
                ins=[sh.ap().opt()], outs=[fl.ap().opt()]),
                reads=(sh.name,), writes=(fl.name,), sem=csem)
        while True:
            yield

    build_iter = _build_gen()

    def emit_build(n):
        for _ in range(n):
            next(build_iter)

    S.cop("vector", lambda e: e.memset(kmax2[:, :], 0.0), writes=("kmax2",))
    S.cop("vector", lambda e: e.memset(epsv[:, :], LN_EPS), writes=("epsv",))

    _dbgidx = [0]

    def ln_front(src_res, src_ap_fn, sl, sq=None, sq_res="xsq"):
        S.cop("vector", lambda e, sl=sl: e.reduce_sum(
            s1v[:, sl:sl + 1], src_ap_fn(), AX.X),
            reads=(src_res,), writes=(f"s1-{sl}",), sync_out=True)
        sq = xsq if sq is None else sq
        S.cop("scalar", lambda e, sl=sl, sq=sq: e.activation(
            sq[:, :], src_ap_fn(), AF.Square, accum_out=s2v[:, sl:sl + 1]),
            reads=(src_res,), writes=(sq_res, f"s2{sl}"))

    def ln_back(src_res, src_ap_fn, dst_ap, sl, out_res):
        S.cop("vector", lambda e, sl=sl: e.tensor_scalar(
            mvec[:, sl:sl + 1], s1v[:, sl:sl + 1], 1.0 / D, None, ALU.mult),
            reads=(f"s1-{sl}",), writes=(f"mv{sl}",), sync_out=True)
        S.cop("vector", lambda e, sl=sl: e.tensor_scalar(
            vvar[:, sl:sl + 1], s2v[:, sl:sl + 1], 1.0 / D, None, ALU.mult),
            reads=(f"s2{sl}",), writes=(f"vv{sl}",), sync_out=True)
        S.cop("vector", lambda e, sl=sl: e.scalar_tensor_tensor(
            msqn[:, :], mvec[:, sl:sl + 1], -1.0, mvec[:, sl:sl + 1],
            ALU.mult, ALU.mult), reads=(f"mv{sl}",), writes=("msqn",), sync_out=True)
        S.cop("vector", lambda e, sl=sl: e.tensor_tensor(
            vvar[:, sl:sl + 1], vvar[:, sl:sl + 1], msqn[:, :], ALU.add),
            reads=("msqn", f"vv{sl}"), writes=(f"vv{sl}",), sync_out=True)
        S.cop("scalar", lambda e, sl=sl: e.activation(
            varl[:, sl:sl + 1], vvar[:, sl:sl + 1], AF.Ln,
            bias=epsv[:, :]),
            reads=(f"vv{sl}", "epsv"), writes=(f"vl{sl}",), sync_out=True)
        S.cop("scalar", lambda e, sl=sl: e.activation(
            rstd[:, sl:sl + 1], varl[:, sl:sl + 1], AF.Exp, scale=-0.5),
            reads=(f"vl{sl}",), writes=(f"rs{sl}",), sync_out=True)
        S.cop("vector", lambda e, sl=sl: e.scalar_tensor_tensor(
            nmr[:, sl:sl + 1], mvec[:, sl:sl + 1], -1.0, rstd[:, sl:sl + 1],
            ALU.mult, ALU.mult), reads=(f"mv{sl}", f"rs{sl}"),
            writes=(f"nm{sl}",), sync_out=True)
        if _DBG:
            di = _dbgidx[0]; _dbgidx[0] += 1
            if di < 48:
                S.cop("vector", lambda e, sl=sl, di=di: e.tensor_copy(
                    dstats[:, 3 * di:3 * di + 1], mvec[:, sl:sl + 1]),
                    reads=(f"mv{sl}",), writes=("dstats",))
                S.cop("vector", lambda e, sl=sl, di=di: e.tensor_copy(
                    dstats[:, 3 * di + 1:3 * di + 2], rstd[:, sl:sl + 1]),
                    reads=(f"rs{sl}",), writes=("dstats",))
                S.cop("vector", lambda e, sl=sl, di=di: e.tensor_copy(
                    dstats[:, 3 * di + 2:3 * di + 3], nmr[:, sl:sl + 1]),
                    reads=(f"nm{sl}",), writes=("dstats",))
        S.cop("scalar", lambda e, sl=sl: e.activation(
            dst_ap, src_ap_fn(), AF.Identity, bias=nmr[:, sl:sl + 1],
            scale=rstd[:, sl:sl + 1]),
            reads=(src_res, f"nm{sl}", f"rs{sl}"), writes=(out_res,))

    def ln_stats_apply(src_res, src_ap_fn, dst_ap, sl, out_res,
                       sq=None, sq_res="xsq"):
        ln_front(src_res, src_ap_fn, sl, sq=sq, sq_res=sq_res)
        ln_back(src_res, src_ap_fn, dst_ap, sl, out_res)

    def tp16(src_res, src_base_ap_fn, dst_fn, tag):
        for g in range(4):
            tb = "T0" if g % 2 == 0 else "T1"
            fns = []
            for u in range(4):
                hc = 4 * g + u
                fns.append(lambda e, hc=hc, u=u, tb=tb: e.transpose(
                    tbank_bf16(tb)[:, 128 * u:128 * (u + 1)],
                    src_base_ap_fn(hc), ident[:, :]))
            S.grp("tensor", fns, reads=(src_res, "ident"), writes=(tb,))
            dst_ap, dst_res = dst_fn(g)
            S.cop("vector", lambda e, tb=tb, dst_ap=dst_ap: e.tensor_copy(
                dst_ap, tbank_bf16(tb)[:, 0:512]),
                reads=(tb,), writes=(dst_res,))

    def xin_ap(xsl):
        return xin3[:, :] if xsl == 2 else xin[:, D * xsl:D * (xsl + 1)]

    def ph0_A(ti):
        j, kind = ti // 2, ("own" if ti % 2 == 0 else "oth")
        sl = ti % 2
        xsl = ti % 3
        src = xq if kind == "own" else xo
        S.dma("sync", xin_ap(xsl), src[128 * j:128 * (j + 1), :],
              writes=(f"xinS{xsl}",))
        ln_front(f"xinS{xsl}",
                 lambda xsl=xsl: xin_ap(xsl), sl)

    def ph0_B(ti):
        sl = ti % 2
        xsl = ti % 3
        ln_back(f"xinS{xsl}",
                lambda xsl=xsl: xin_ap(xsl),
                lnt[:, D * sl:D * (sl + 1)], sl, f"lnt{sl}")

    def ph0_C(ti):
        j, kind = ti // 2, ("own" if ti % 2 == 0 else "oth")
        sl = ti % 2
        if kind == "own":
            def dst_fn(g, j=j):
                col0 = D * (4 * g) + 128 * j
                ap = bass.AP(ln1o, col0, [[NHC * D, 128], [D, 4], [1, 128]])
                return ap, "ln1o"
        else:
            def dst_fn(g, j=j):
                col0 = 256 * (4 * g) + 128 * (j % 2)
                ap = bass.AP(lno, col0, [[NHC * 256, 128], [256, 4], [1, 128]])
                return ap, f"lno{j % 2}"
        tp16(f"lnt{sl}",
             lambda hc, sl=sl: lnt[:, D * sl + 128 * hc:D * sl + 128 * (hc + 1)],
             dst_fn, f"ln1tp{ti}")

        kt = 2 * j + (0 if kind == "own" else 1)
        kb = "B0" if ti % 2 == 0 else "B1"
        vb = "B2" if ti % 2 == 0 else "B3"
        if kind == "own":
            rhs_res = "ln1o"

            def rhs_ap(hc, j=j):
                return ln1o[:, D * hc + 128 * j:D * hc + 128 * (j + 1)]
        else:
            rhs_res = f"lno{j % 2}"

            def rhs_ap(hc, j=j):
                return lno[:, 256 * hc + 128 * (j % 2):
                           256 * hc + 128 * (j % 2) + 128]
        S.grp("tensor",
              [lambda e, hc=hc, kb=kb, rhs_ap=rhs_ap: e.matmul(
                  bap(kb)[:, 0:128], wkvb[:, 256 * hc:256 * hc + 128],
                  rhs_ap(hc), start=(hc == 0), stop=(hc == NHC - 1))
               for hc in range(NHC)],
              reads=("wkvb", rhs_res), writes=(kb,))
        S.grp("tensor",
              [lambda e, hc=hc, vb=vb, rhs_ap=rhs_ap: e.matmul(
                  bap(vb)[:, 0:128], rhs_ap(hc),
                  wkvb[:, 256 * hc + 128:256 * (hc + 1)],
                  start=(hc == 0), stop=(hc == NHC - 1))
               for hc in range(NHC)],
              reads=("wkvb", rhs_res), writes=(vb,))
        S.cop("scalar", lambda e, kt=kt, kb=kb: e.copy(
            K[:, 128 * kt:128 * (kt + 1)], bap(kb)[:, 0:128]),
            reads=(kb,), writes=("K",))
        S.cop("vector", lambda e, kt=kt, vb=vb: e.tensor_copy(
            V[:, 128 * kt:128 * (kt + 1)], bap(vb)[:, 0:128]),
            reads=(vb,), writes=("V",))
        S.cop("vector", lambda e, kt=kt: e.tensor_tensor(
            sqsc[:, 0:128], K[:, 128 * kt:128 * (kt + 1)],
            K[:, 128 * kt:128 * (kt + 1)], ALU.mult),
            reads=("K",), writes=("sqsc",))
        nb = "Py" if ti % 2 == 0 else "Pd"
        S.grp("tensor", [lambda e, nb=nb: e.matmul(
            bap(nb)[:, 0:128], onesb[:, :], sqsc[:, 0:128],
            start=True, stop=True)], reads=("onesb", "sqsc"), writes=(nb,))
        S.cop("vector", lambda e, nb=nb: e.reduce_max(
            kqmax[:, 0:1], bap(nb)[:, 0:128], AX.X),
            reads=(nb,), writes=("kqmax",), sync_out=True)
        S.cop("vector", lambda e: e.tensor_max(
            kmax2[:, :], kmax2[:, :], kqmax[:, 0:1]),
            reads=("kqmax", "kmax2"), writes=("kmax2",), sync_out=True)

    for t in range(34):
        if 1 <= t < 33:
            ph0_B(t - 1)
        if t < 32:
            ph0_A(t)
        if t >= 2:
            ph0_C(t - 2)

    if _DBG:
        S.dma("scalar", dbg["d_stats"].ap(), dstats[:, :],
              reads=("dstats",))
        S.dma("scalar", dbg["d_ln1o"].ap(), ln1o[:, :], reads=("ln1o",))
        S.dma("scalar", dbg["d_K"].ap(), K[:, :], reads=("K",))
        S.dma("scalar", dbg["d_V"].ap(), V[:, :], reads=("V",))
        S.dma("scalar", dbg["d_kmax2"].ap(), kmax2[:, :], reads=("kmax2",))

    S.cop("scalar", lambda e: e.activation(lnkv[:, :], kmax2[:, :], AF.Ln),
          reads=("kmax2",), writes=("lnkv",), sync_out=True)

    def load_wq(h):
        sl = h % 2
        for hc in range(NHC):
            S.dma("gpsimd", wqb[:, D * sl + 128 * hc:D * sl + 128 * (hc + 1)],
                  wqT[128 * hc:128 * (hc + 1), 128 * h:128 * (h + 1)],
                  grp=("wq", h),
                  writes=((f"wqb{sl}",) if hc == NHC - 1 else ()),
                  war=((f"wqb{sl}",) if hc == 0 else ()))
    load_wq(0)
    load_wq(1)

    S.alias["ytm"] = ["xinS0", "xinS1", "xinS2", "xsq", "lnt0", "lnt1",
                      "lno0", "lno1",
                      "wkvb"]
    NKT = {0: 8, 1: 16, 2: 24, 3: 32}

    def emit_preamble(h):
        sl = h % 2
        for nch in range(4):
            qb = f"B{nch}"
            S.grp("tensor",
                  [lambda e, hc=hc, sl=sl, nch=nch, qb=qb: e.matmul(
                      bap(qb)[:, :],
                      wqb[:, D * sl + 128 * hc:D * sl + 128 * (hc + 1)],
                      ln1o[:, D * hc + 512 * nch:D * hc + 512 * (nch + 1)],
                      start=(hc == 0), stop=(hc == NHC - 1))
                   for hc in range(NHC)],
                  reads=(f"wqb{sl}", "ln1o"), writes=(qb,))
            S.cop("scalar", lambda e, sl=sl, nch=nch, qb=qb: e.activation(
                Qp[:, D * sl + 512 * nch:D * sl + 512 * (nch + 1)],
                bap(qb)[:, :], AF.Identity, scale=RS2),
                reads=(qb,), writes=(f"Qp{sl}",))
            S.cop("vector", lambda e, sl=sl, nch=nch: e.tensor_tensor(
                sqsc[:, :], Qp[:, D * sl + 512 * nch:D * sl + 512 * (nch + 1)],
                Qp[:, D * sl + 512 * nch:D * sl + 512 * (nch + 1)], ALU.mult),
                reads=(f"Qp{sl}",), writes=("sqsc",))
            nb = "T0" if nch % 2 == 0 else "T1"
            S.grp("tensor", [lambda e, nb=nb: e.matmul(
                bap(nb)[:, :], onesb[:, :], sqsc[:, :], start=True, stop=True)],
                reads=("onesb", "sqsc"), writes=(nb,))
            S.cop("vector", lambda e, nb=nb, nch=nch: e.reduce_max(
                kqmax[:, nch:nch + 1], bap(nb)[:, :], AX.X),
                reads=(nb,), writes=("kqmax",), sync_out=True)
            S.cop("scalar", lambda e, nch=nch: e.activation(
                t0v[:, nch:nch + 1], kqmax[:, nch:nch + 1], AF.Ln),
                reads=("kqmax",), writes=(f"t0-{nch}",), sync_out=True)
            S.cop("vector", lambda e, nch=nch: e.tensor_tensor(
                t0v[:, nch:nch + 1], t0v[:, nch:nch + 1], lnkv[:, :], ALU.add),
                reads=(f"t0-{nch}", "lnkv"), writes=(f"t0-{nch}",),
                sync_out=True)
            S.cop("scalar", lambda e, nch=nch: e.activation(
                negb[:, nch:nch + 1], t0v[:, nch:nch + 1], AF.Exp, scale=0.5),
                reads=(f"t0-{nch}",), writes=(f"negb{nch}",), sync_out=True)
            S.cop("vector", lambda e, nch=nch: e.tensor_scalar(
                negb[:, nch:nch + 1], negb[:, nch:nch + 1], -1.01, 39.5,
                ALU.mult, ALU.add),
                reads=(f"negb{nch}",), writes=(f"negb{nch}",), sync_out=True)
        if h + 2 < H:
            load_wq(h + 2)

    pend_epi = [None]
    emit_preamble(0)
    for h in range(H):
        sl = h % 2
        for qc in range(4):
            nkt = NKT[qc]
            ngrp = nkt // 4

            def emit_scores(g, qc=qc, sl=sl):
                for u in range(4):
                    i = 4 * g + u
                    kt = i
                    o = max(128 * (kt // 2) - 512 * qc, 0)
                    bk = f"B{i % 4}"
                    S.grp("tensor", [lambda e, bk=bk, o=o, kt=kt, sl=sl, qc=qc:
                                     e.matmul(bap(bk)[:, o:512],
                                              K[:, 128 * kt:128 * (kt + 1)],
                                              Qp[:, D * sl + 512 * qc + o:
                                                 D * sl + 512 * (qc + 1)],
                                              start=True, stop=True)],
                          reads=("K", f"Qp{sl}"), writes=(bk,))

            def emit_exps(g, qc=qc):
                sb8 = (g % 2) * 4
                if 4 * g + 3 <= 8 * qc - 1:
                    S.cop("scalar", lambda e, g=g, qc=qc: e.activation(
                        PTb[:, 2048 * (g % 2):2048 * (g % 2) + 2048],
                        Psc[:, 0:2048], AF.Exp, bias=negb[:, qc:qc + 1]),
                        reads=("B0", "B1", "B2", "B3", f"negb{qc}"),
                        writes=tuple(f"PT{sb8 + u}" for u in range(4)))
                    return
                for pp in range(2):
                    i0 = 4 * g + 2 * pp
                    o = max(128 * (i0 // 2) - 512 * qc, 0)
                    n = 512 - o
                    in_ap = bass.AP(Psc, 1024 * pp + o,
                                    [[2048, 128], [512, 2], [1, n]])
                    out_ap = bass.AP(PTb, 2048 * (g % 2) + 1024 * pp + o,
                                     [[4096, 128], [512, 2], [1, n]])
                    S.cop("scalar", lambda e, in_ap=in_ap, out_ap=out_ap,
                          qc=qc: e.activation(out_ap, in_ap, AF.Exp,
                                              bias=negb[:, qc:qc + 1]),
                          reads=(f"B{2 * pp}", f"B{2 * pp + 1}", f"negb{qc}"),
                          writes=(f"PT{sb8 + 2 * pp}", f"PT{sb8 + 2 * pp + 1}"))
                for u in range(4):
                    i = 4 * g + u
                    kt = i
                    if (kt // 2) < 4 * qc:
                        continue
                    o = max(128 * (kt // 2) - 512 * qc, 0)
                    s8 = sb8 + (i % 4)
                    mt, mres = (tril, "tril") if kt % 2 == 0 else (maskb, "maskb")
                    S.cop("vector", lambda e, s8=s8, o=o, mt=mt:
                          e.tensor_tensor(
                              PTb[:, 512 * s8 + o:512 * s8 + o + 128],
                              PTb[:, 512 * s8 + o:512 * s8 + o + 128],
                              mt[:, :], ALU.mult),
                          reads=(f"PT{s8}", mres), writes=(f"PT{s8}",))

            def emit_avden(g, qc=qc, nkt=nkt):
                sb8 = (g % 2) * 4
                fns = []
                rds = set(["V", "onesb"])
                for u in range(4):
                    i = 4 * g + u
                    kt = i
                    o = max(128 * (kt // 2) - 512 * qc, 0)
                    s8 = sb8 + (i % 4)
                    rds.add(f"PT{s8}")
                    fns.append(lambda e, kt=kt, s8=s8, i=i, o=o, nkt=nkt:
                               e.matmul(
                                   P["Py"][:, o:512],
                                   V[:, 128 * kt:128 * (kt + 1)],
                                   PTb[:, 512 * s8 + o:512 * (s8 + 1)],
                                   start=(i == 0), stop=(i == nkt - 1)))
                    fns.append(lambda e, s8=s8, i=i, o=o, nkt=nkt: e.matmul(
                        P["Pd"][:, o:512], onesb[:, :],
                        PTb[:, 512 * s8 + o:512 * (s8 + 1)],
                        start=(i == 0), stop=(i == nkt - 1)))
                S.grp("tensor", fns, reads=tuple(rds), writes=("Py", "Pd"))

            prev = None
            for g in range(ngrp):
                emit_scores(g)
                emit_exps(g)
                if g == 0 and pend_epi[0] is not None:
                    pend_epi[0]()
                    pend_epi[0] = None
                if prev is not None:
                    emit_avden(prev)
                if qc == 3 and g == ngrp - 1 and h + 1 < H:
                    emit_preamble(h + 1)
                emit_build(3)
                prev = g
            emit_avden(prev)

            def mk_epi(qc=qc, h=h):
                def go():
                    tb = "T0" if qc % 2 == 0 else "T1"
                    S.cop("scalar", lambda e, qc=qc: e.copy(
                        yfm[:, 512 * qc:512 * (qc + 1)], P["Py"][:, :]),
                        reads=("Py",), writes=("yfm",))
                    S.cop("vector", lambda e: e.tensor_copy(
                        denrb[:, :], P["Pd"][:, :]),
                        reads=("Pd",), writes=("denrb",), sync_out=True)
                    fns = [lambda e, jq=jq, qc=qc, tb=tb: e.transpose(
                        tbank_bf16(tb)[:, 128 * jq:128 * (jq + 1)],
                        yfm[:, 512 * qc + 128 * jq:512 * qc + 128 * (jq + 1)],
                        ident[:, :]) for jq in range(4)]
                    fns += [lambda e, jq=jq, tb=tb: e.transpose(
                        tbank_bf16(tb)[:, 512 + 128 * jq:512 + 128 * (jq + 1)],
                        denrb[:, 128 * jq:128 * (jq + 1)],
                        ident[:, :]) for jq in range(4)]
                    S.grp("tensor", fns, reads=("yfm", "denrb", "ident"),
                          writes=(tb,))
                    S.cop("vector", lambda e, qc=qc, tb=tb: e.tensor_copy(
                        denTr[:, 4 * qc:4 * qc + 4],
                        tbank_bf16(tb)[:, 512:1024].rearrange(
                            "p (j i) -> p j i", i=128)[:, :, 0:1]),
                        reads=(tb,), writes=("denTr",), sync_out=True)
                    S.cop("vector", lambda e, qc=qc: e.reciprocal(
                        denT[:, 4 * qc:4 * qc + 4], denTr[:, 4 * qc:4 * qc + 4]),
                        reads=("denTr",), writes=("denT",), sync_out=True)
                    for jq in range(4):
                        qt = 4 * qc + jq
                        S.cop("vector", lambda e, jq=jq, qt=qt, h=h, tb=tb:
                              e.tensor_scalar(
                                  ytm[:, D * qt + 128 * h:
                                      D * qt + 128 * (h + 1)],
                                  tbank_bf16(tb)[:, 128 * jq:128 * (jq + 1)],
                                  denT[:, qt:qt + 1], None, ALU.mult),
                              reads=(tb, "denT"), writes=("ytm",))
                return go
            pend_epi[0] = mk_epi()
    if pend_epi[0] is not None:
        pend_epi[0]()
        pend_epi[0] = None
        if _DBG and h == 0:
            S.dma("scalar", dbg["d_qp0"].ap(), Qp[:, 0:D], reads=("Qp0",))
            S.dma("scalar", dbg["d_wqb0"].ap(), wqb[:, 0:D],
                  reads=("wqb0",))
            S.dma("scalar", dbg["d_negb0"].ap(), negb[:, :],
                  reads=tuple(f"negb{i}" for i in range(4)))
            S.dma("scalar", dbg["d_denT0"].ap(), denT[:, :], reads=("denT",))
            S.dma("scalar", dbg["d_yfm0"].ap(), yfm[:, :], reads=("yfm",))

    if _DBG:
        S.dma("scalar", dbg["d_ytm"].ap(), ytm[:, :], reads=("ytm",))

    for r in ("xin2-0", "xin2-1", "x2t0", "x2t1", "ln2tm0", "ln2tm1",
              "ln2sb0", "ln2sb1", "xsq2"):
        S.alias[r] = ["ln1o"]

    for r in ("urhs", "urhsB", "ln2ts", "x2r0", "x2r1", "x2r2", "x2r3",
              "wdrsA0", "wdrsA1", "wdrsB0", "wdrsB1"):
        S.alias[r] = ["ln1o", "xin2-0", "xin2-1", "x2t0", "x2t1",
                      "ln2tm0", "ln2tm1", "ln2sb0", "ln2sb1", "xsq2"]
    S.alias["hact"] = ["ytm"]
    S.alias["grhs"] = ["ytm"]
    S.alias["grhsB"] = ["ytm"]
    NI5 = 11

    done_ld = set()

    def emit_ln2ts(rt):
        if rt > 3 or ("ln2ts", rt) in done_ld:
            return
        done_ld.add(("ln2ts", rt))
        for hc in range(NHC):
            srcap = bass.AP(ln2_scr, 4 * rt * (128 * 2048) + 128 * hc,
                            [[2048, 128], [128 * 2048, 4], [1, 128]])
            S.dma("sync", ln2ts[:, 512 * hc:512 * (hc + 1)],
                  srcap, grp=("ln2ts", rt),
                  reads=tuple(f"ln2s{4 * rt + s}" for s in range(4)),
                  war=(("ln2ts",) if hc == 0 else ()),
                  writes=(("ln2ts",) if hc == NHC - 1 else ()))

    def emit_gfill(rt, i5):
        if rt > 3 or ("g", rt, i5) in done_ld:
            return
        done_ld.add(("g", rt, i5))
        iw = 512 if i5 < NI5 - 1 else ITER - 512 * (NI5 - 1)
        for hc in range(NHC):
            half = "grhs" if hc % 2 == 0 else "grhsB"
            S.dma("sync" if hc % 2 == 0 else "gpsimd",
                  grhs[:, 512 * hc:512 * hc + iw],
                  g_full[128 * hc:128 * (hc + 1), 512 * i5:512 * i5 + iw],
                  grp=("g", rt, i5, hc % 2), reads=("g_full",),
                  war=((half,) if hc < 2 else ()),
                  writes=((half,) if hc >= NHC - 2 else ()))

    def emit_ufill(rt, i5):
        if rt > 3 or ("u", rt, i5) in done_ld:
            return
        done_ld.add(("u", rt, i5))
        iw = 512 if i5 < NI5 - 1 else ITER - 512 * (NI5 - 1)
        for hc in range(NHC):
            half = "urhs" if hc % 2 == 0 else "urhsB"
            S.dma("sync" if hc % 2 == 0 else "gpsimd",
                  urhs[:, 512 * hc:512 * hc + iw],
                  u_full[128 * hc:128 * (hc + 1), 512 * i5:512 * i5 + iw],
                  grp=("u", rt, i5, hc % 2), reads=("u_full",),
                  war=((half,) if hc < 2 else ()),
                  writes=((half,) if hc >= NHC - 2 else ()))

    def ph2c_A(st):
        sl = st % 2
        S.dma("sync", xin2[:, D * sl:D * (sl + 1)],
              xq[128 * st:128 * (st + 1), :], writes=(f"xin2-{sl}",))
        S.cop("vector", lambda e, sl=sl, st=st: e.tensor_tensor(
            x2t[:, D * sl:D * (sl + 1)], xin2[:, D * sl:D * (sl + 1)],
            ytm[:, D * st:D * (st + 1)], ALU.add),
            reads=(f"xin2-{sl}", "ytm"), writes=(f"x2t{sl}",))
        S.dma("gpsimd", x2_scr[st:st + 1, :].rearrange("o (p d) -> (o p) d",
                                                       p=128),
              x2t[:, D * sl:D * (sl + 1)], grp="x2scr",
              reads=(f"x2t{sl}",), writes=("x2_scr",))
        ln_front(f"x2t{sl}",
                 lambda sl=sl: x2t[:, D * sl:D * (sl + 1)], sl,
                 sq=xsq2, sq_res="xsq2")

    def ph2c_B(st):
        sl = st % 2
        ln_back(f"x2t{sl}",
                lambda sl=sl: x2t[:, D * sl:D * (sl + 1)],
                ln2tm[:, D * sl:D * (sl + 1)], sl, f"ln2tm{sl}")

    def ph2c_C(st):
        sl = st % 2

        def dst_fn(g, sl=sl):
            col0 = NHC * 128 * sl + 128 * (4 * g)
            ap = bass.AP(ln2sb, col0, [[2 * NHC * 128, 128], [128, 4], [1, 128]])
            return ap, f"ln2sb{sl}"
        tp16(f"ln2tm{sl}",
             lambda hc, sl=sl: ln2tm[:, D * sl + 128 * hc:D * sl + 128 * (hc + 1)],
             dst_fn, f"ln2tp{st}")
        S.dma("gpsimd",
              ln2_scr[st:st + 1, :].rearrange("o (p d) -> (o p) d", p=128),
              ln2sb[:, NHC * 128 * sl:NHC * 128 * (sl + 1)], grp="ln2scr",
              reads=(f"ln2sb{sl}",), writes=(f"ln2s{st}",))

    for t in range(18):
        if 1 <= t < 17:
            ph2c_B(t - 1)
        if t < 16:
            ph2c_A(t)
        if t == 16:
            emit_build(len(tiles) + 1)
            for csem in ("C0", "C1", "C2"):
                S._wait("sync", csem, 1)
                S._wait("gpsimd", csem, 1)
            emit_gfill(0, 0)
        if t >= 2:
            ph2c_C(t - 2)

    BS4 = (("B0", "B1", "B2", "B3"), ("Py", "Pd", "T0", "T1"))
    for rt in range(4):
        emit_ln2ts(rt)
        for i5 in range(NI5):
            iw = 512 if i5 < NI5 - 1 else ITER - 512 * (NI5 - 1)
            nib = iw // 128
            emit_gfill(rt, i5)
            for ib in range(nib):
                gb = f"B{ib}"
                ic = 4 * i5 + ib
                S.grp("tensor",
                      [lambda e, hc=hc, ib=ib, gb=gb: e.matmul(
                          bap(gb)[:, :],
                          grhs[:, 512 * hc + 128 * ib:512 * hc + 128 * (ib + 1)],
                          ln2ts[:, 512 * hc:512 * (hc + 1)],
                          start=(hc == 0), stop=(hc == NHC - 1))
                       for hc in range(NHC)],
                      reads=("ln2ts", "grhs", "grhsB"), writes=(gb,))
                S.cop("scalar", lambda e, ic=ic, gb=gb:
                      e.activation(hact[:, 512 * ic:512 * (ic + 1)],
                                   bap(gb)[:, :], AF.Silu),
                      reads=(gb,), writes=("hact",))
            emit_ufill(rt, i5)
            for ib in range(nib):
                ub = f"B{ib}"
                ic = 4 * i5 + ib
                S.grp("tensor",
                      [lambda e, hc=hc, ib=ib, ub=ub: e.matmul(
                          bap(ub)[:, :],
                          urhs[:, 512 * hc + 128 * ib:512 * hc + 128 * (ib + 1)],
                          ln2ts[:, 512 * hc:512 * (hc + 1)],
                          start=(hc == 0), stop=(hc == NHC - 1))
                       for hc in range(NHC)],
                      reads=("ln2ts", "urhs", "urhsB"), writes=(ub,))
                S.cop("vector", lambda e, ic=ic, ub=ub:
                      e.tensor_tensor(
                          hact[:, 512 * ic:512 * (ic + 1)],
                          hact[:, 512 * ic:512 * (ic + 1)],
                          bap(ub)[:, :], ALU.mult),
                      reads=(ub, "hact"), writes=("hact",))
        if _DBG and rt == 0:
            S.dma("scalar", dbg["d_hact0"].ap(), hact[:, :],
                  reads=("hact",))
        emit_ln2ts(rt + 1)
        emit_gfill(rt + 1, 0)
        emit_ufill(rt + 1, 0)
        for ph in range(2):
            for rb in range(4):
                st = rt * 4 + rb
                S.dma("sync",
                      x2rl[:, 1024 * rb:1024 * (rb + 1)],
                      bass.AP(x2_scr, st * (128 * D) + 1024 * ph,
                              [[D, 128], [1, 1024]]),
                      grp="x2r",
                      reads=("x2_scr",), writes=(f"x2r{rb}",))
            for ic in range(NIC):
                wsl = ic % 2
                wres = (f"wdrsA{wsl}", f"wdrsB{wsl}")
                S.dma("sync", wdrs[:, 1024 * wsl:1024 * wsl + 512],
                      d_full[128 * ic:128 * (ic + 1),
                             1024 * ph:1024 * ph + 512],
                      grp=("wd", wsl, 0),
                      reads=("d_full",), writes=(wres[0],))
                S.dma("gpsimd", wdrs[:, 1024 * wsl + 512:1024 * (wsl + 1)],
                      d_full[128 * ic:128 * (ic + 1),
                             1024 * ph + 512:1024 * (ph + 1)],
                      grp=("wd", wsl, 1),
                      reads=("d_full",), writes=(wres[1],))
                fns = []
                for rb in range(4):
                    for hh in range(2):
                        fns.append(lambda e, rb=rb, ic=ic, hh=hh, wsl=wsl,
                                   bk=BANKS8[2 * rb + hh]: e.matmul(
                                       bap(bk)[:, :],
                                       hact[:, 512 * ic + 128 * rb:
                                            512 * ic + 128 * (rb + 1)],
                                       wdrs[:, 1024 * wsl + 512 * hh:
                                            1024 * wsl + 512 * (hh + 1)],
                                       start=(ic == 0), stop=(ic == NIC - 1)))
                S.grp("tensor", fns,
                      reads=("hact",) + wres,
                      writes=tuple(BANKS8))
            for rb in range(4):
                st = rt * 4 + rb
                for hh in range(2):
                    bk = BANKS8[2 * rb + hh]
                    S.cop("vector", lambda e, rb=rb, hh=hh, bk=bk:
                          e.tensor_tensor(
                              x2rl[:, 1024 * rb + 512 * hh:
                                   1024 * rb + 512 * (hh + 1)],
                              x2rl[:, 1024 * rb + 512 * hh:
                                   1024 * rb + 512 * (hh + 1)],
                              bap(bk)[:, :], ALU.add),
                          reads=(bk, f"x2r{rb}"),
                          writes=(f"x2r{rb}",))
                S.dma("scalar",
                      y_out[128 * st:128 * (st + 1),
                            1024 * ph:1024 * (ph + 1)],
                      x2rl[:, 1024 * rb:1024 * (rb + 1)],
                      grp="yout",
                      reads=(f"x2r{rb}",))

    if _DBG:
        S.dma("gpsimd", dbg["d_x2"].ap(), x2_scr.ap(), reads=("x2_scr",))
        S.dma("gpsimd", dbg["d_ln2"].ap(), ln2_scr.ap(),
              reads=("ln2_scr",))
        S.dma("gpsimd", dbg["d_gfull"].ap(), g_full[0:128, :],
              reads=("g_full",))
        S.dma("gpsimd", dbg["d_dfull"].ap(), d_full[0:128, :],
              reads=("d_full",))

    S.drain_dmas("scalar", queues=("scalar",))
    S.drain_dmas("sync", queues=("sync",))
    S.drain_dmas("gpsimd", queues=("gpsimd",))
    S.verify()

    with nc.Block() as block:
        @block.sync
        def _(e):
            for item in S.q["sync"]:
                item[3](e)

        @block.scalar
        def _(e):
            for item in S.q["scalar"]:
                item[3](e)

        @block.vector
        def _(e):
            for item in S.q["vector"]:
                item[3](e)

        @block.tensor
        def _(e):
            for item in S.q["tensor"]:
                item[3](e)

        @block.gpsimd
        def _(e):
            for item in S.q["gpsimd"]:
                item[3](e)

    nc.reset()
    nc.finalize()
    return nc


_NC_CACHE = {}


def kernel(L0_attn_x, L0_attn_w_kv, L0_attn_w_q, L0_ffn_lut, L0_ffn_gate_idx,
           L0_ffn_up_idx, L0_ffn_down_idx):
    x = np.asarray(L0_attn_x, dtype=np.float32)
    w_kv = np.asarray(L0_attn_w_kv, dtype=np.float32)
    w_q = np.asarray(L0_attn_w_q, dtype=np.float32)
    lut = np.asarray(L0_ffn_lut, dtype=np.float32)
    g_i = np.asarray(L0_ffn_gate_idx, dtype=np.int32)
    u_i = np.asarray(L0_ffn_up_idx, dtype=np.int32)
    d_i = np.asarray(L0_ffn_down_idx, dtype=np.int32)

    if "nc" not in _NC_CACHE:
        _NC_CACHE["nc"] = build_nc()
    nc = _NC_CACHE["nc"]

    wqT_h = np.ascontiguousarray(w_q.T)
    wkvT_h = np.ascontiguousarray(w_kv.T)
    lutb_h = np.ascontiguousarray(np.broadcast_to(lut[None, :], (128, 16)))
    gT = g_i.T
    uT = u_i.T
    dT = d_i.T

    in_maps = make_in_maps(x, wqT_h, wkvT_h, lutb_h, gT, uT, dT)
    res = run_bass_kernel_spmd(nc, in_maps, core_ids=list(range(8)))
    out = np.empty((4, T, D), dtype=np.float32)
    for c in range(8):
        b, half = c // 2, c % 2
        yo = np.asarray(res.results[c]["y_out"]).reshape(16, 128, D)
        for j in range(16):
            r0 = 128 * (2 * j + half)
            out[b, r0:r0 + 128, :] = yo[j]
    return out


def make_in_maps(x, wqT_h, wkvT_h, lutb_h, gT, uT, dT):
    in_maps = []
    for c in range(8):
        b, half = c // 2, c % 2
        xb = x[b].reshape(32, 128, D)
        own = np.ascontiguousarray(
            xb[[2 * j + half for j in range(16)]]).reshape(2048, D)
        oth = np.ascontiguousarray(
            xb[[2 * j + 1 - half for j in range(16)]]).reshape(2048, D)
        mb = np.full((128, 128), float(half), dtype=ml_dtypes.bfloat16)
        in_maps.append({
            "xq": own, "xo": oth, "wqT": wqT_h, "wkvT": wkvT_h, "lutb": lutb_h,
            "gidx": np.ascontiguousarray(gT[256 * c:256 * (c + 1)]),
            "uidx": np.ascontiguousarray(uT[256 * c:256 * (c + 1)]),
            "didx": np.ascontiguousarray(dT[688 * c:688 * (c + 1)]),
            "maskb": mb,
        })
    return in_maps


# version 30
# speedup vs baseline: 1.0334x; 1.0095x over previous
import math
import os
import numpy as np
import ml_dtypes
import concourse.bass as bass
import concourse.mybir as mybir
from concourse.bass_utils import run_bass_kernel_spmd

F32 = mybir.dt.float32
BF16 = mybir.dt.bfloat16
I32 = mybir.dt.int32
AF = mybir.ActivationFunctionType
ALU = mybir.AluOpType
AX = mybir.AxisListType

_DBG = bool(int(os.environ.get("MK_DBG", "0")))
D = 2048
T = 4096
H = 16
ITER = 5504
NIC = ITER // 128
NHC = D // 128
RS2 = 1.0 / math.sqrt(128.0)
LN_EPS = 1e-5

ENGINES = ("sync", "scalar", "vector", "tensor", "gpsimd")
ENG_SEM = {"tensor": "PE", "scalar": "AC", "vector": "DV", "gpsimd": "PL"}
COMPUTE_SEMS = {"PE", "AC", "DV", "PL"}


class Sched:

    FORCE_SYNC_OUT = bool(int(os.environ.get("MK_SYNCALL", "1")))

    DQ = {"sync": ("LD", 16), "scalar": ("LS", 4), "gpsimd": ("LG", 4)}

    def __init__(self, sems):
        self.sems = sems
        self.q = {e: [] for e in ENGINES}
        self.cnt = {s: 0 for s in sems}
        self.res = {}
        self.hw = {}
        self.alias = {}
        self.dma_rot = {e: 0 for e in self.DQ}
        self.dma_grp = {}

    def _wait(self, eng, sem, val):
        if val <= 0:
            return
        key = (eng, sem)
        if self.hw.get(key, 0) >= val:
            return
        self.hw[key] = val
        s = self.sems[sem]
        self.q[eng].append(("w", sem, val,
                            lambda e, s=s, v=val: e.wait_ge(s, v)))

    def raw(self, eng, fn):
        self.q[eng].append(("o", None, None, fn))

    def _entry_wait(self, eng, entry, is_dma, war=False):
        sem, val, peng = entry
        if (not is_dma) and (not war) and peng == eng and sem in COMPUTE_SEMS:
            return
        if is_dma and peng == eng and sem not in COMPUTE_SEMS:
            return
        self._wait(eng, sem, val)

    def _pre(self, eng, reads, writes, is_dma):
        for r in writes:
            for stale in self.alias.pop(r, ()):
                st = self.res.get(stale)
                if st:
                    if st["w"]:
                        self._entry_wait(eng, st["w"], is_dma)
                    for e2 in st["r"].values():
                        self._entry_wait(eng, e2, is_dma, war=True)
        for r in reads:
            st = self.res.get(r)
            if st and st["w"]:
                self._entry_wait(eng, st["w"], is_dma)
        for r in writes:
            st = self.res.get(r)
            if st:
                if st["w"]:
                    self._entry_wait(eng, st["w"], is_dma)
                for e2 in st["r"].values():
                    self._entry_wait(eng, e2, is_dma, war=True)

    def _post(self, eng, reads, writes, sem, val, sync_out=False):
        if self.FORCE_SYNC_OUT:
            sync_out = True
        for r in writes:
            self.res[r] = {"w": (sem, val, None if sync_out else eng), "r": {}}
        for r in reads:
            st = self.res.setdefault(r, {"w": None, "r": {}})
            k = (sem, eng)
            if st["r"].get(k, (None, 0, None))[1] < val:
                st["r"][k] = (sem, val, eng)

    def cop(self, eng, fn, reads=(), writes=(), sem=None, k=1,
            sync_out=False):
        sem = sem or ENG_SEM[eng]
        self._pre(eng, reads, writes, False)
        self.cnt[sem] += k
        val = self.cnt[sem]
        s = self.sems[sem]
        self.q[eng].append(("i", sem, k,
                            lambda e, fn=fn, s=s, k=k: fn(e).then_inc(s, k)))
        self._post(eng, reads, writes, sem, val, sync_out)
        return val

    def grp(self, eng, fns, reads=(), writes=()):
        sem = ENG_SEM[eng]
        self._pre(eng, reads, writes, False)
        for fn in fns[:-1]:
            self.raw(eng, fn)
        self.cnt[sem] += 1
        val = self.cnt[sem]
        s = self.sems[sem]
        self.q[eng].append(("i", sem, 1,
                            lambda e, fn=fns[-1], s=s: fn(e).then_inc(s, 1)))
        self._post(eng, reads, writes, sem, val)
        return val

    def dma(self, eng, out, in_, sem=None, reads=(), writes=(), war=(),
            grp=None):
        base, M = self.DQ[eng]
        if grp is None or grp not in self.dma_grp:
            i = self.dma_rot[eng] % M
            self.dma_rot[eng] += 1
            psem = f"{base}{i}"
            if grp is not None:
                self.dma_grp[grp] = psem
            self._wait(eng, psem, self.cnt[psem])
        else:
            psem = self.dma_grp[grp]
        self._pre(eng, reads, tuple(writes) + tuple(war), True)
        self.cnt[psem] += 16
        val = self.cnt[psem]
        s = self.sems[psem]
        self.q[eng].append(("i", psem, 16,
                            lambda e, out=out, in_=in_, s=s:
                            e.dma_start(out=out, in_=in_).then_inc(s, 16)))
        self._post(eng, reads, writes, psem, val)
        return psem, val

    def drain_dmas(self, eng_wait="scalar", queues=("sync", "scalar",
                                                    "gpsimd")):
        for qeng in queues:
            base, M = self.DQ[qeng]
            for i in range(M):
                psem = f"{base}{i}"
                self._wait(eng_wait, psem, self.cnt[psem])

    def verify(self):
        pos = {e: 0 for e in ENGINES}
        vals = {s: 0 for s in self.sems}
        progress = True
        while progress:
            progress = False
            for e in ENGINES:
                while pos[e] < len(self.q[e]):
                    kind, sem, v, _ = self.q[e][pos[e]]
                    if kind == "w" and vals[sem] < v:
                        break
                    if kind == "i":
                        vals[sem] += v
                    pos[e] += 1
                    progress = True
        stuck = {e: (pos[e], len(self.q[e])) for e in ENGINES
                 if pos[e] < len(self.q[e])}
        if stuck:
            detail = {}
            for e in stuck:
                kind, sem, v, _ = self.q[e][pos[e]]
                detail[e] = (pos[e], kind, sem, v,
                             vals.get(sem) if sem else None)
            raise RuntimeError(f"schedule deadlock: {detail}")


def build_nc():
    nc = bass.Bass("TRN2", target_bir_lowering=False)

    xq = nc.dram_tensor("xq", [2048, D], F32, kind="ExternalInput")
    xo = nc.dram_tensor("xo", [2048, D], F32, kind="ExternalInput")
    wqT = nc.dram_tensor("wqT", [D, D], F32, kind="ExternalInput")
    wkvT = nc.dram_tensor("wkvT", [D, 256], F32, kind="ExternalInput")
    lutb = nc.dram_tensor("lutb", [128, 16], F32, kind="ExternalInput")
    gidx = nc.dram_tensor("gidx", [256, ITER], I32, kind="ExternalInput")
    uidx = nc.dram_tensor("uidx", [256, ITER], I32, kind="ExternalInput")
    didx = nc.dram_tensor("didx", [688, D], I32, kind="ExternalInput")
    maskb_d = nc.dram_tensor("maskb", [128, 128], BF16, kind="ExternalInput")
    y_out = nc.dram_tensor("y_out", [2048, D], F32, kind="ExternalOutput")

    g_shard = nc.dram_tensor("g_shard", [256, ITER], BF16, kind="Internal")
    u_shard = nc.dram_tensor("u_shard", [256, ITER], BF16, kind="Internal")
    d_shard = nc.dram_tensor("d_shard", [688, D], BF16, kind="Internal")
    g_full = nc.dram_tensor("g_full", [D, ITER], BF16, kind="Internal", addr_space="Shared")
    u_full = nc.dram_tensor("u_full", [D, ITER], BF16, kind="Internal", addr_space="Shared")
    d_full = nc.dram_tensor("d_full", [ITER, D], BF16, kind="Internal", addr_space="Shared")
    x2_scr = nc.dram_tensor("x2_scr", [16, 128 * D], F32, kind="Internal")
    ln2_scr = nc.dram_tensor("ln2_scr", [16, 128 * 2048], BF16, kind="Internal")
    dbg = {}
    dstats = None
    if _DBG:
        dstats = nc.alloc_sbuf_tensor("dstats", [128, 3 * 48], F32)
        for nm, shape, dty in (("d_ln1o", [128, NHC * D], BF16),
                               ("d_K", [128, T], BF16),
                               ("d_V", [128, T], BF16),
                               ("d_kmax2", [128, 1], F32),
                               ("d_qp0", [128, D], BF16),
                               ("d_wqb0", [128, D], BF16),
                               ("d_negb0", [128, 4], F32),
                               ("d_denT0", [128, 16], F32),
                               ("d_yfm0", [128, D], BF16),
                               ("d_ytm", [128, 16 * D], BF16),
                               ("d_x2", [16, 128 * D], F32),
                               ("d_ln2", [16, 128 * 2048], BF16),
                               ("d_gfull", [128, ITER], BF16),
                               ("d_dfull", [128, D], BF16),
                               ("d_hact0", [128, 4 * ITER], BF16),
                               ("d_stats", [128, 3 * 48], F32)):
            dbg[nm] = nc.dram_tensor(nm, shape, dty, kind="ExternalOutput")

    tri = np.zeros((128, 128), dtype=np.float32)
    tri[np.triu_indices(128)] = 1.0
    ident_b = nc.inline_tensor(np.eye(128, dtype=ml_dtypes.bfloat16), name="c_ib")
    ident_f = nc.inline_tensor(np.eye(128, dtype=np.float32), name="c_if")
    tril_b = nc.inline_tensor(tri.astype(ml_dtypes.bfloat16), name="c_tri")
    ones_b = nc.inline_tensor(np.ones((128, 128), dtype=ml_dtypes.bfloat16),
                              name="c_on")

    def sb(name, shape, dtype):
        return nc.alloc_sbuf_tensor(name, shape, dtype)

    ident = sb("ident", [128, 128], BF16)
    identf = sb("identf", [128, 128], F32)
    tril = sb("tril", [128, 128], BF16)
    maskb = sb("maskb_s", [128, 128], BF16)
    onesb = sb("onesb_s", [128, 128], BF16)
    lutbs = sb("lutbs", [128, 16], F32)
    K = sb("K", [128, T], BF16)
    V = sb("V", [128, T], BF16)

    Qp = sb("Qp", [128, 2 * D], BF16)
    wqb = sb("wqb", [128, 2 * D], BF16)
    sqsc = sb("sqsc", [128, 512], BF16)
    PTb = sb("PTb", [128, 8 * 512], BF16)
    denrb = sb("denrb", [128, 512], BF16)
    denTr = sb("denTr", [128, 16], F32)
    denT = sb("denT", [128, 16], F32)
    yfm = sb("yfm", [128, D], BF16)
    kqmax = sb("kqmax", [128, 8], F32)
    kmax2 = sb("kmax2", [128, 1], F32)
    qmax2 = sb("qmax2", [128, 1], F32)
    negb = sb("negb", [128, 4], F32)
    t0v = sb("t0v", [128, 4], F32)
    lnkv = sb("lnkv", [128, 1], F32)
    s1v = sb("s1v", [128, 2], F32)
    s2v = sb("s2v", [128, 2], F32)
    mvec = sb("mvec", [128, 2], F32)
    vvar = sb("vvar", [128, 2], F32)
    varl = sb("varl", [128, 2], F32)
    rstd = sb("rstd", [128, 2], F32)
    nmr = sb("nmr", [128, 2], F32)
    msqn = sb("msqn", [128, 1], F32)
    epsv = sb("epsv", [128, 1], F32)
    upsc = sb("upsc", [128, 512], BF16)
    bidx = sb("bidx", [128, 2 * 512], I32)
    bidxb = sb("bidxb", [128, 512], BF16)
    bt = sb("bt", [128, 512], BF16)
    bacc = sb("bacc", [128, 2 * 512], BF16)
    bacc2 = sb("bacc2", [128, 2 * 512], BF16)
    bt2 = sb("bt2", [128, 512], BF16)

    base = (nc.sbuf_base + 31) & ~31
    R1 = base
    R2 = R1 + 65536
    assert R2 + 65536 <= 229344 - 64, f"SBUF overflow {R2 + 65536}"

    def sba(name, shape, dtype, off):
        return nc.alloc_sbuf_tensor_at(name, shape, dtype, offset=off)

    ln1o = sba("ln1o", [128, NHC * D], BF16, R1)
    xin2 = sba("xin2", [128, 2 * D], F32, R1)
    x2t = sba("x2t", [128, 2 * D], F32, R1 + 16384)
    ln2tm = sba("ln2tm", [128, 2 * D], BF16, R1 + 32768)
    ln2sb = sba("ln2sb", [128, 2 * NHC * 128], BF16, R1 + 40960)
    xsq2 = sba("xsq2", [128, D], BF16, R1 + 49152)
    xin23 = sba("xin23", [128, D], F32, R1 + 53248)
    urhs = sba("urhs", [128, NHC * 512], BF16, R1)
    ln2ts = sba("ln2ts", [128, NHC * 512], BF16, R1 + 16384)
    x2rl = sba("x2rl", [128, 2 * D], F32, R1 + 32768)
    wdrs = sba("wdrs", [128, 2 * D], BF16, R1 + 49152)
    xin = sba("xin", [128, 2 * D], F32, R2)
    xsq = sba("xsq", [128, D], BF16, R2 + 16384)
    lnt = sba("lnt", [128, 2 * D], BF16, R2 + 20480)
    lno = sba("lno", [128, NHC * 256], BF16, R2 + 28672)
    wkvb = sba("wkvb", [128, NHC * 256], BF16, R2 + 36864)
    xin3 = sba("xin3", [128, D], F32, R2 + 45056)
    ytm = sba("ytm", [128, 16 * D], BF16, R2)
    hact = sba("hact", [128, 4 * ITER], BF16, R2)
    grhs = sba("grhs", [128, NHC * 512], BF16, R2 + 44032)

    Psc = nc.alloc_psum_tensor("Psc", [128, 4 * 512], F32)
    P = {}
    for name in ("Py", "Pd", "T0", "T1"):
        P[name] = nc.alloc_psum_tensor(name, [128, 512], F32)
    BANKS8 = ("B0", "B1", "B2", "B3", "Py", "Pd", "T0", "T1")

    def bap(name):
        if name.startswith("B"):
            i = int(name[1])
            return Psc[:, 512 * i:512 * (i + 1)]
        return P[name][:, :]

    def tbank_bf16(name):
        return P[name].ap().bitcast(BF16)

    sems = {}
    names = ["PE", "AC", "DV", "PL", "C0", "C1", "C2"]
    names += [f"LD{i}" for i in range(16)]
    names += [f"LS{i}" for i in range(4)]
    names += [f"LG{i}" for i in range(4)]
    for s in names:
        sems[s] = nc.alloc_semaphore(name=s)
    S = Sched(sems)

    for dst, src, rs in ((ident, ident_b, "ident"), (identf, ident_f, "identf"),
                         (tril, tril_b, "tril"), (maskb, maskb_d, "maskb"),
                         (onesb, ones_b, "onesb"), (lutbs, lutb, "lutbs")):
        S.dma("sync", dst[:, :], src.ap(), writes=(rs,))
    for hc in range(NHC):
        S.dma("gpsimd", wkvb[:, 256 * hc:256 * (hc + 1)],
              wkvT[128 * hc:128 * (hc + 1), :], grp="wkv",
              writes=("wkvb",) if hc == NHC - 1 else ())

    tiles = []
    for (src, dst, nrows, ncols) in ((gidx, g_shard, 256, ITER),
                                     (uidx, u_shard, 256, ITER),
                                     (didx, d_shard, 688, D)):
        p0 = 0
        while p0 < nrows:
            pr = min(128, nrows - p0)
            c0 = 0
            while c0 < ncols:
                cw = min(512, ncols - c0)
                tiles.append((src, dst, p0, pr, c0, cw))
                c0 += cw
            p0 += pr

    NPOOL_L = 0

    def emit_build_tile_steps(i):
        src_, dst, p0, pr, c0, cw = tiles[i]
        sl = i % 2
        S.dma("sync", bidx[0:pr, 512 * sl:512 * sl + cw],
              src_[p0:p0 + pr, c0:c0 + cw], writes=(f"bidx{sl}",))
        S.cop("vector", lambda e, pr=pr, cw=cw, sl=sl: e.tensor_copy(
            bidxb[0:pr, 0:cw], bidx[0:pr, 512 * sl:512 * sl + cw]),
            reads=(f"bidx{sl}", "lutbs"), writes=("bidxb",))
        yield
        for l in range(16 - NPOOL_L):
            if l == 0:
                S.cop("vector", lambda e, pr=pr, cw=cw, sl=sl, l=l:
                      e.tensor_scalar(bacc[0:pr, 512 * sl:512 * sl + cw],
                                      bidxb[0:pr, 0:cw], float(l),
                                      lutbs[0:pr, l:l + 1],
                                      ALU.is_equal, ALU.mult),
                      reads=("bidxb",), writes=(f"bacc{sl}",))
            else:
                S.cop("vector", lambda e, pr=pr, cw=cw, l=l:
                      e.tensor_scalar(bt[0:pr, 0:cw], bidxb[0:pr, 0:cw],
                                      float(l), lutbs[0:pr, l:l + 1],
                                      ALU.is_equal, ALU.mult),
                      reads=("bidxb",), writes=("bt",))
                S.cop("vector", lambda e, pr=pr, cw=cw, sl=sl:
                      e.tensor_tensor(bacc[0:pr, 512 * sl:512 * sl + cw],
                                      bacc[0:pr, 512 * sl:512 * sl + cw],
                                      bt[0:pr, 0:cw], ALU.add),
                      reads=("bt", f"bacc{sl}"), writes=(f"bacc{sl}",))
            if l % 2 == 1:
                yield
        for l in range(16 - NPOOL_L, 16) if NPOOL_L else ():
            if l == 16 - NPOOL_L:
                S.cop("gpsimd", lambda e, pr=pr, cw=cw, sl=sl, l=l:
                      e.tensor_scalar(bacc2[0:pr, 512 * sl:512 * sl + cw],
                                      bidxb[0:pr, 0:cw], float(l),
                                      lutbs[0:pr, l:l + 1],
                                      ALU.is_equal, ALU.mult),
                      reads=("bidxb",), writes=(f"bacc2{sl}",))
            else:
                S.cop("gpsimd", lambda e, pr=pr, cw=cw, l=l:
                      e.tensor_scalar(bt2[0:pr, 0:cw], bidxb[0:pr, 0:cw],
                                      float(l), lutbs[0:pr, l:l + 1],
                                      ALU.is_equal, ALU.mult),
                      reads=("bidxb",), writes=("bt2",))
                S.cop("gpsimd", lambda e, pr=pr, cw=cw, sl=sl:
                      e.tensor_tensor(bacc2[0:pr, 512 * sl:512 * sl + cw],
                                      bacc2[0:pr, 512 * sl:512 * sl + cw],
                                      bt2[0:pr, 0:cw], ALU.add),
                      reads=("bt2", f"bacc2{sl}"), writes=(f"bacc2{sl}",))
            yield
        if NPOOL_L:
            S.cop("vector", lambda e, pr=pr, cw=cw, sl=sl:
                  e.tensor_tensor(bacc[0:pr, 512 * sl:512 * sl + cw],
                                  bacc[0:pr, 512 * sl:512 * sl + cw],
                                  bacc2[0:pr, 512 * sl:512 * sl + cw],
                                  ALU.add),
                  reads=(f"bacc2{sl}", f"bacc{sl}"), writes=(f"bacc{sl}",))
        S.dma("gpsimd", dst[p0:p0 + pr, c0:c0 + cw],
              bacc[0:pr, 512 * sl:512 * sl + cw], grp=("shard", dst.name),
              reads=(f"bacc{sl}",), writes=(dst.name,))
        yield

    def _build_gen():
        for i in range(len(tiles)):
            for step in emit_build_tile_steps(i):
                yield
        for sh, fl, csem in ((g_shard, g_full, "C0"),
                             (u_shard, u_full, "C1"),
                             (d_shard, d_full, "C2")):
            S.cop("gpsimd", lambda e, sh=sh, fl=fl: e.collective_compute(
                "AllGather", ALU.bypass, replica_groups=[list(range(8))],
                ins=[sh.ap().opt()], outs=[fl.ap().opt()]),
                reads=(sh.name,), writes=(fl.name,), sem=csem)
        while True:
            yield

    build_iter = _build_gen()

    def emit_build(n):
        for _ in range(n):
            next(build_iter)

    S.cop("vector", lambda e: e.memset(kmax2[:, :], 0.0), writes=("kmax2",))
    S.cop("vector", lambda e: e.memset(epsv[:, :], LN_EPS), writes=("epsv",))

    _dbgidx = [0]

    def ln_front(src_res, src_ap_fn, sl, sq=None, sq_res="xsq"):
        S.cop("vector", lambda e, sl=sl: e.reduce_sum(
            s1v[:, sl:sl + 1], src_ap_fn(), AX.X),
            reads=(src_res,), writes=(f"s1-{sl}",), sync_out=True)
        sq = xsq if sq is None else sq
        S.cop("scalar", lambda e, sl=sl, sq=sq: e.activation(
            sq[:, :], src_ap_fn(), AF.Square, accum_out=s2v[:, sl:sl + 1]),
            reads=(src_res,), writes=(sq_res, f"s2{sl}"))

    def ln_back(src_res, src_ap_fn, dst_ap, sl, out_res):
        S.cop("vector", lambda e, sl=sl: e.tensor_scalar(
            mvec[:, sl:sl + 1], s1v[:, sl:sl + 1], 1.0 / D, None, ALU.mult),
            reads=(f"s1-{sl}",), writes=(f"mv{sl}",), sync_out=True)
        S.cop("vector", lambda e, sl=sl: e.tensor_scalar(
            vvar[:, sl:sl + 1], s2v[:, sl:sl + 1], 1.0 / D, None, ALU.mult),
            reads=(f"s2{sl}",), writes=(f"vv{sl}",), sync_out=True)
        S.cop("vector", lambda e, sl=sl: e.scalar_tensor_tensor(
            msqn[:, :], mvec[:, sl:sl + 1], -1.0, mvec[:, sl:sl + 1],
            ALU.mult, ALU.mult), reads=(f"mv{sl}",), writes=("msqn",), sync_out=True)
        S.cop("vector", lambda e, sl=sl: e.tensor_tensor(
            vvar[:, sl:sl + 1], vvar[:, sl:sl + 1], msqn[:, :], ALU.add),
            reads=("msqn", f"vv{sl}"), writes=(f"vv{sl}",), sync_out=True)
        S.cop("scalar", lambda e, sl=sl: e.activation(
            varl[:, sl:sl + 1], vvar[:, sl:sl + 1], AF.Ln,
            bias=epsv[:, :]),
            reads=(f"vv{sl}", "epsv"), writes=(f"vl{sl}",), sync_out=True)
        S.cop("scalar", lambda e, sl=sl: e.activation(
            rstd[:, sl:sl + 1], varl[:, sl:sl + 1], AF.Exp, scale=-0.5),
            reads=(f"vl{sl}",), writes=(f"rs{sl}",), sync_out=True)
        S.cop("vector", lambda e, sl=sl: e.scalar_tensor_tensor(
            nmr[:, sl:sl + 1], mvec[:, sl:sl + 1], -1.0, rstd[:, sl:sl + 1],
            ALU.mult, ALU.mult), reads=(f"mv{sl}", f"rs{sl}"),
            writes=(f"nm{sl}",), sync_out=True)
        if _DBG:
            di = _dbgidx[0]; _dbgidx[0] += 1
            if di < 48:
                S.cop("vector", lambda e, sl=sl, di=di: e.tensor_copy(
                    dstats[:, 3 * di:3 * di + 1], mvec[:, sl:sl + 1]),
                    reads=(f"mv{sl}",), writes=("dstats",))
                S.cop("vector", lambda e, sl=sl, di=di: e.tensor_copy(
                    dstats[:, 3 * di + 1:3 * di + 2], rstd[:, sl:sl + 1]),
                    reads=(f"rs{sl}",), writes=("dstats",))
                S.cop("vector", lambda e, sl=sl, di=di: e.tensor_copy(
                    dstats[:, 3 * di + 2:3 * di + 3], nmr[:, sl:sl + 1]),
                    reads=(f"nm{sl}",), writes=("dstats",))
        S.cop("scalar", lambda e, sl=sl: e.activation(
            dst_ap, src_ap_fn(), AF.Identity, bias=nmr[:, sl:sl + 1],
            scale=rstd[:, sl:sl + 1]),
            reads=(src_res, f"nm{sl}", f"rs{sl}"), writes=(out_res,))

    def ln_stats_apply(src_res, src_ap_fn, dst_ap, sl, out_res,
                       sq=None, sq_res="xsq"):
        ln_front(src_res, src_ap_fn, sl, sq=sq, sq_res=sq_res)
        ln_back(src_res, src_ap_fn, dst_ap, sl, out_res)

    def tp16(src_res, src_base_ap_fn, dst_fn, tag):
        for g in range(4):
            tb = "T0" if g % 2 == 0 else "T1"
            fns = []
            for u in range(4):
                hc = 4 * g + u
                fns.append(lambda e, hc=hc, u=u, tb=tb: e.transpose(
                    tbank_bf16(tb)[:, 128 * u:128 * (u + 1)],
                    src_base_ap_fn(hc), ident[:, :]))
            S.grp("tensor", fns, reads=(src_res, "ident"), writes=(tb,))
            dst_ap, dst_res = dst_fn(g)
            S.cop("vector", lambda e, tb=tb, dst_ap=dst_ap: e.tensor_copy(
                dst_ap, tbank_bf16(tb)[:, 0:512]),
                reads=(tb,), writes=(dst_res,))

    def xin_ap(xsl):
        return xin3[:, :] if xsl == 2 else xin[:, D * xsl:D * (xsl + 1)]

    def ph0_A(ti):
        j, kind = ti // 2, ("own" if ti % 2 == 0 else "oth")
        sl = ti % 2
        xsl = ti % 3
        src = xq if kind == "own" else xo
        S.dma("sync", xin_ap(xsl), src[128 * j:128 * (j + 1), :],
              writes=(f"xinS{xsl}",))
        ln_front(f"xinS{xsl}",
                 lambda xsl=xsl: xin_ap(xsl), sl)

    def ph0_B(ti):
        sl = ti % 2
        xsl = ti % 3
        ln_back(f"xinS{xsl}",
                lambda xsl=xsl: xin_ap(xsl),
                lnt[:, D * sl:D * (sl + 1)], sl, f"lnt{sl}")

    def ph0_C(ti):
        j, kind = ti // 2, ("own" if ti % 2 == 0 else "oth")
        sl = ti % 2
        if kind == "own":
            def dst_fn(g, j=j):
                col0 = D * (4 * g) + 128 * j
                ap = bass.AP(ln1o, col0, [[NHC * D, 128], [D, 4], [1, 128]])
                return ap, "ln1o"
        else:
            def dst_fn(g, j=j):
                col0 = 256 * (4 * g) + 128 * (j % 2)
                ap = bass.AP(lno, col0, [[NHC * 256, 128], [256, 4], [1, 128]])
                return ap, f"lno{j % 2}"
        tp16(f"lnt{sl}",
             lambda hc, sl=sl: lnt[:, D * sl + 128 * hc:D * sl + 128 * (hc + 1)],
             dst_fn, f"ln1tp{ti}")

        kt = 2 * j + (0 if kind == "own" else 1)
        kb = "B0" if ti % 2 == 0 else "B1"
        vb = "B2" if ti % 2 == 0 else "B3"
        if kind == "own":
            rhs_res = "ln1o"

            def rhs_ap(hc, j=j):
                return ln1o[:, D * hc + 128 * j:D * hc + 128 * (j + 1)]
        else:
            rhs_res = f"lno{j % 2}"

            def rhs_ap(hc, j=j):
                return lno[:, 256 * hc + 128 * (j % 2):
                           256 * hc + 128 * (j % 2) + 128]
        S.grp("tensor",
              [lambda e, hc=hc, kb=kb, rhs_ap=rhs_ap: e.matmul(
                  bap(kb)[:, 0:128], wkvb[:, 256 * hc:256 * hc + 128],
                  rhs_ap(hc), start=(hc == 0), stop=(hc == NHC - 1))
               for hc in range(NHC)],
              reads=("wkvb", rhs_res), writes=(kb,))
        S.grp("tensor",
              [lambda e, hc=hc, vb=vb, rhs_ap=rhs_ap: e.matmul(
                  bap(vb)[:, 0:128], rhs_ap(hc),
                  wkvb[:, 256 * hc + 128:256 * (hc + 1)],
                  start=(hc == 0), stop=(hc == NHC - 1))
               for hc in range(NHC)],
              reads=("wkvb", rhs_res), writes=(vb,))
        S.cop("scalar", lambda e, kt=kt, kb=kb: e.copy(
            K[:, 128 * kt:128 * (kt + 1)], bap(kb)[:, 0:128]),
            reads=(kb,), writes=("K",))
        S.cop("vector", lambda e, kt=kt, vb=vb: e.tensor_copy(
            V[:, 128 * kt:128 * (kt + 1)], bap(vb)[:, 0:128]),
            reads=(vb,), writes=("V",))
        S.cop("vector", lambda e, kt=kt: e.tensor_tensor(
            sqsc[:, 0:128], K[:, 128 * kt:128 * (kt + 1)],
            K[:, 128 * kt:128 * (kt + 1)], ALU.mult),
            reads=("K",), writes=("sqsc",))
        nb = "Py" if ti % 2 == 0 else "Pd"
        S.grp("tensor", [lambda e, nb=nb: e.matmul(
            bap(nb)[:, 0:128], onesb[:, :], sqsc[:, 0:128],
            start=True, stop=True)], reads=("onesb", "sqsc"), writes=(nb,))
        S.cop("vector", lambda e, nb=nb: e.reduce_max(
            kqmax[:, 0:1], bap(nb)[:, 0:128], AX.X),
            reads=(nb,), writes=("kqmax",), sync_out=True)
        S.cop("vector", lambda e: e.tensor_max(
            kmax2[:, :], kmax2[:, :], kqmax[:, 0:1]),
            reads=("kqmax", "kmax2"), writes=("kmax2",), sync_out=True)

    for t in range(34):
        if 1 <= t < 33:
            ph0_B(t - 1)
        if t < 32:
            ph0_A(t)
        if t >= 2:
            ph0_C(t - 2)

    if _DBG:
        S.dma("scalar", dbg["d_stats"].ap(), dstats[:, :],
              reads=("dstats",))
        S.dma("scalar", dbg["d_ln1o"].ap(), ln1o[:, :], reads=("ln1o",))
        S.dma("scalar", dbg["d_K"].ap(), K[:, :], reads=("K",))
        S.dma("scalar", dbg["d_V"].ap(), V[:, :], reads=("V",))
        S.dma("scalar", dbg["d_kmax2"].ap(), kmax2[:, :], reads=("kmax2",))

    S.cop("scalar", lambda e: e.activation(lnkv[:, :], kmax2[:, :], AF.Ln),
          reads=("kmax2",), writes=("lnkv",), sync_out=True)

    def load_wq(h):
        sl = h % 2
        for hc in range(NHC):
            S.dma("gpsimd", wqb[:, D * sl + 128 * hc:D * sl + 128 * (hc + 1)],
                  wqT[128 * hc:128 * (hc + 1), 128 * h:128 * (h + 1)],
                  grp=("wq", h),
                  writes=((f"wqb{sl}",) if hc == NHC - 1 else ()),
                  war=((f"wqb{sl}",) if hc == 0 else ()))
    load_wq(0)
    load_wq(1)

    S.alias["ytm"] = ["xinS0", "xinS1", "xinS2", "xsq", "lnt0", "lnt1",
                      "lno0", "lno1",
                      "wkvb"]
    NKT = {0: 8, 1: 16, 2: 24, 3: 32}

    def emit_preamble(h):
        sl = h % 2
        for nch in range(4):
            qb = f"B{nch}"
            S.grp("tensor",
                  [lambda e, hc=hc, sl=sl, nch=nch, qb=qb: e.matmul(
                      bap(qb)[:, :],
                      wqb[:, D * sl + 128 * hc:D * sl + 128 * (hc + 1)],
                      ln1o[:, D * hc + 512 * nch:D * hc + 512 * (nch + 1)],
                      start=(hc == 0), stop=(hc == NHC - 1))
                   for hc in range(NHC)],
                  reads=(f"wqb{sl}", "ln1o"), writes=(qb,))
            S.cop("scalar", lambda e, sl=sl, nch=nch, qb=qb: e.activation(
                Qp[:, D * sl + 512 * nch:D * sl + 512 * (nch + 1)],
                bap(qb)[:, :], AF.Identity, scale=RS2),
                reads=(qb,), writes=(f"Qp{sl}",))
            S.cop("vector", lambda e, sl=sl, nch=nch: e.tensor_tensor(
                sqsc[:, :], Qp[:, D * sl + 512 * nch:D * sl + 512 * (nch + 1)],
                Qp[:, D * sl + 512 * nch:D * sl + 512 * (nch + 1)], ALU.mult),
                reads=(f"Qp{sl}",), writes=("sqsc",))
            nb = "T0" if nch % 2 == 0 else "T1"
            S.grp("tensor", [lambda e, nb=nb: e.matmul(
                bap(nb)[:, :], onesb[:, :], sqsc[:, :], start=True, stop=True)],
                reads=("onesb", "sqsc"), writes=(nb,))
            S.cop("vector", lambda e, nb=nb, nch=nch: e.reduce_max(
                kqmax[:, nch:nch + 1], bap(nb)[:, :], AX.X),
                reads=(nb,), writes=("kqmax",), sync_out=True)
            S.cop("scalar", lambda e, nch=nch: e.activation(
                t0v[:, nch:nch + 1], kqmax[:, nch:nch + 1], AF.Ln),
                reads=("kqmax",), writes=(f"t0-{nch}",), sync_out=True)
            S.cop("vector", lambda e, nch=nch: e.tensor_tensor(
                t0v[:, nch:nch + 1], t0v[:, nch:nch + 1], lnkv[:, :], ALU.add),
                reads=(f"t0-{nch}", "lnkv"), writes=(f"t0-{nch}",),
                sync_out=True)
            S.cop("scalar", lambda e, nch=nch: e.activation(
                negb[:, nch:nch + 1], t0v[:, nch:nch + 1], AF.Exp, scale=0.5),
                reads=(f"t0-{nch}",), writes=(f"negb{nch}",), sync_out=True)
            S.cop("vector", lambda e, nch=nch: e.tensor_scalar(
                negb[:, nch:nch + 1], negb[:, nch:nch + 1], -1.01, 39.5,
                ALU.mult, ALU.add),
                reads=(f"negb{nch}",), writes=(f"negb{nch}",), sync_out=True)
        if h + 2 < H:
            load_wq(h + 2)

    pend_epi = [None]
    emit_preamble(0)
    for h in range(H):
        sl = h % 2
        for qc in range(4):
            nkt = NKT[qc]
            ngrp = nkt // 4

            def emit_scores(g, qc=qc, sl=sl):
                for u in range(4):
                    i = 4 * g + u
                    kt = i
                    o = max(128 * (kt // 2) - 512 * qc, 0)
                    bk = f"B{i % 4}"
                    S.grp("tensor", [lambda e, bk=bk, o=o, kt=kt, sl=sl, qc=qc:
                                     e.matmul(bap(bk)[:, o:512],
                                              K[:, 128 * kt:128 * (kt + 1)],
                                              Qp[:, D * sl + 512 * qc + o:
                                                 D * sl + 512 * (qc + 1)],
                                              start=True, stop=True)],
                          reads=("K", f"Qp{sl}"), writes=(bk,))

            def emit_exps(g, qc=qc):
                sb8 = (g % 2) * 4
                if 4 * g + 3 <= 8 * qc - 1:
                    S.cop("scalar", lambda e, g=g, qc=qc: e.activation(
                        PTb[:, 2048 * (g % 2):2048 * (g % 2) + 2048],
                        Psc[:, 0:2048], AF.Exp, bias=negb[:, qc:qc + 1]),
                        reads=("B0", "B1", "B2", "B3", f"negb{qc}"),
                        writes=tuple(f"PT{sb8 + u}" for u in range(4)))
                    return
                for pp in range(2):
                    i0 = 4 * g + 2 * pp
                    o = max(128 * (i0 // 2) - 512 * qc, 0)
                    n = 512 - o
                    in_ap = bass.AP(Psc, 1024 * pp + o,
                                    [[2048, 128], [512, 2], [1, n]])
                    out_ap = bass.AP(PTb, 2048 * (g % 2) + 1024 * pp + o,
                                     [[4096, 128], [512, 2], [1, n]])
                    S.cop("scalar", lambda e, in_ap=in_ap, out_ap=out_ap,
                          qc=qc: e.activation(out_ap, in_ap, AF.Exp,
                                              bias=negb[:, qc:qc + 1]),
                          reads=(f"B{2 * pp}", f"B{2 * pp + 1}", f"negb{qc}"),
                          writes=(f"PT{sb8 + 2 * pp}", f"PT{sb8 + 2 * pp + 1}"))
                for u in range(4):
                    i = 4 * g + u
                    kt = i
                    if (kt // 2) < 4 * qc:
                        continue
                    o = max(128 * (kt // 2) - 512 * qc, 0)
                    s8 = sb8 + (i % 4)
                    mt, mres = (tril, "tril") if kt % 2 == 0 else (maskb, "maskb")
                    S.cop("vector", lambda e, s8=s8, o=o, mt=mt:
                          e.tensor_tensor(
                              PTb[:, 512 * s8 + o:512 * s8 + o + 128],
                              PTb[:, 512 * s8 + o:512 * s8 + o + 128],
                              mt[:, :], ALU.mult),
                          reads=(f"PT{s8}", mres), writes=(f"PT{s8}",))

            def emit_avden(g, qc=qc, nkt=nkt):
                sb8 = (g % 2) * 4
                fns = []
                rds = set(["V", "onesb"])
                for u in range(4):
                    i = 4 * g + u
                    kt = i
                    o = max(128 * (kt // 2) - 512 * qc, 0)
                    s8 = sb8 + (i % 4)
                    rds.add(f"PT{s8}")
                    fns.append(lambda e, kt=kt, s8=s8, i=i, o=o, nkt=nkt:
                               e.matmul(
                                   P["Py"][:, o:512],
                                   V[:, 128 * kt:128 * (kt + 1)],
                                   PTb[:, 512 * s8 + o:512 * (s8 + 1)],
                                   start=(i == 0), stop=(i == nkt - 1)))
                    fns.append(lambda e, s8=s8, i=i, o=o, nkt=nkt: e.matmul(
                        P["Pd"][:, o:512], onesb[:, :],
                        PTb[:, 512 * s8 + o:512 * (s8 + 1)],
                        start=(i == 0), stop=(i == nkt - 1)))
                S.grp("tensor", fns, reads=tuple(rds), writes=("Py", "Pd"))

            prev = None
            for g in range(ngrp):
                emit_scores(g)
                emit_exps(g)
                if g == 0 and pend_epi[0] is not None:
                    pend_epi[0]()
                    pend_epi[0] = None
                if prev is not None:
                    emit_avden(prev)
                if qc == 3 and g == ngrp - 1 and h + 1 < H:
                    emit_preamble(h + 1)
                emit_build(3)
                prev = g
            emit_avden(prev)

            def mk_epi(qc=qc, h=h):
                def go():
                    tb = "T0" if qc % 2 == 0 else "T1"
                    S.cop("scalar", lambda e, qc=qc: e.copy(
                        yfm[:, 512 * qc:512 * (qc + 1)], P["Py"][:, :]),
                        reads=("Py",), writes=("yfm",))
                    S.cop("vector", lambda e: e.tensor_copy(
                        denrb[:, :], P["Pd"][:, :]),
                        reads=("Pd",), writes=("denrb",), sync_out=True)
                    fns = [lambda e, jq=jq, qc=qc, tb=tb: e.transpose(
                        tbank_bf16(tb)[:, 128 * jq:128 * (jq + 1)],
                        yfm[:, 512 * qc + 128 * jq:512 * qc + 128 * (jq + 1)],
                        ident[:, :]) for jq in range(4)]
                    fns += [lambda e, jq=jq, tb=tb: e.transpose(
                        tbank_bf16(tb)[:, 512 + 128 * jq:512 + 128 * (jq + 1)],
                        denrb[:, 128 * jq:128 * (jq + 1)],
                        ident[:, :]) for jq in range(4)]
                    S.grp("tensor", fns, reads=("yfm", "denrb", "ident"),
                          writes=(tb,))
                    S.cop("vector", lambda e, qc=qc, tb=tb: e.tensor_copy(
                        denTr[:, 4 * qc:4 * qc + 4],
                        tbank_bf16(tb)[:, 512:1024].rearrange(
                            "p (j i) -> p j i", i=128)[:, :, 0:1]),
                        reads=(tb,), writes=("denTr",), sync_out=True)
                    S.cop("vector", lambda e, qc=qc: e.reciprocal(
                        denT[:, 4 * qc:4 * qc + 4], denTr[:, 4 * qc:4 * qc + 4]),
                        reads=("denTr",), writes=("denT",), sync_out=True)
                    for jq in range(4):
                        qt = 4 * qc + jq
                        S.cop("vector", lambda e, jq=jq, qt=qt, h=h, tb=tb:
                              e.tensor_scalar(
                                  ytm[:, D * qt + 128 * h:
                                      D * qt + 128 * (h + 1)],
                                  tbank_bf16(tb)[:, 128 * jq:128 * (jq + 1)],
                                  denT[:, qt:qt + 1], None, ALU.mult),
                              reads=(tb, "denT"), writes=("ytm",))
                return go
            pend_epi[0] = mk_epi()
    if pend_epi[0] is not None:
        pend_epi[0]()
        pend_epi[0] = None
        if _DBG and h == 0:
            S.dma("scalar", dbg["d_qp0"].ap(), Qp[:, 0:D], reads=("Qp0",))
            S.dma("scalar", dbg["d_wqb0"].ap(), wqb[:, 0:D],
                  reads=("wqb0",))
            S.dma("scalar", dbg["d_negb0"].ap(), negb[:, :],
                  reads=tuple(f"negb{i}" for i in range(4)))
            S.dma("scalar", dbg["d_denT0"].ap(), denT[:, :], reads=("denT",))
            S.dma("scalar", dbg["d_yfm0"].ap(), yfm[:, :], reads=("yfm",))

    if _DBG:
        S.dma("scalar", dbg["d_ytm"].ap(), ytm[:, :], reads=("ytm",))

    for r in ("xin2-0", "xin2-1", "xin2-2", "x2t0", "x2t1", "ln2tm0",
              "ln2tm1", "ln2sb0", "ln2sb1", "xsq2"):
        S.alias[r] = ["ln1o"]

    for r in ("urhs", "urhsB", "ln2ts", "x2r0", "x2r1", "x2r2", "x2r3",
              "wdrsA0", "wdrsA1", "wdrsB0", "wdrsB1"):
        S.alias[r] = ["ln1o", "xin2-0", "xin2-1", "xin2-2", "x2t0", "x2t1",
                      "ln2tm0", "ln2tm1", "ln2sb0", "ln2sb1", "xsq2"]
    S.alias["hact"] = ["ytm"]
    S.alias["grhs"] = ["ytm"]
    S.alias["grhsB"] = ["ytm"]
    NI5 = 11

    done_ld = set()

    def emit_ln2ts(rt):
        if rt > 3 or ("ln2ts", rt) in done_ld:
            return
        done_ld.add(("ln2ts", rt))
        for hc in range(NHC):
            srcap = bass.AP(ln2_scr, 4 * rt * (128 * 2048) + 128 * hc,
                            [[2048, 128], [128 * 2048, 4], [1, 128]])
            S.dma("sync", ln2ts[:, 512 * hc:512 * (hc + 1)],
                  srcap, grp=("ln2ts", rt),
                  reads=tuple(f"ln2s{4 * rt + s}" for s in range(4)),
                  war=(("ln2ts",) if hc == 0 else ()),
                  writes=(("ln2ts",) if hc == NHC - 1 else ()))

    def emit_gfill(rt, i5):
        if rt > 3 or ("g", rt, i5) in done_ld:
            return
        done_ld.add(("g", rt, i5))
        iw = 512 if i5 < NI5 - 1 else ITER - 512 * (NI5 - 1)
        for hc in range(NHC):
            half = "grhs" if hc % 2 == 0 else "grhsB"
            S.dma("sync" if hc % 2 == 0 else "gpsimd",
                  grhs[:, 512 * hc:512 * hc + iw],
                  g_full[128 * hc:128 * (hc + 1), 512 * i5:512 * i5 + iw],
                  grp=("g", rt, i5, hc % 2), reads=("g_full",),
                  war=((half,) if hc < 2 else ()),
                  writes=((half,) if hc >= NHC - 2 else ()))

    def emit_ufill(rt, i5):
        if rt > 3 or ("u", rt, i5) in done_ld:
            return
        done_ld.add(("u", rt, i5))
        iw = 512 if i5 < NI5 - 1 else ITER - 512 * (NI5 - 1)
        for hc in range(NHC):
            half = "urhs" if hc % 2 == 0 else "urhsB"
            S.dma("sync" if hc % 2 == 0 else "gpsimd",
                  urhs[:, 512 * hc:512 * hc + iw],
                  u_full[128 * hc:128 * (hc + 1), 512 * i5:512 * i5 + iw],
                  grp=("u", rt, i5, hc % 2), reads=("u_full",),
                  war=((half,) if hc < 2 else ()),
                  writes=((half,) if hc >= NHC - 2 else ()))

    def xin2_ap(xsl):
        return xin23[:, :] if xsl == 2 else xin2[:, D * xsl:D * (xsl + 1)]

    def ph2c_A(st):
        sl = st % 2
        xsl = st % 3
        S.dma("sync", xin2_ap(xsl),
              xq[128 * st:128 * (st + 1), :], writes=(f"xin2-{xsl}",))
        S.cop("vector", lambda e, sl=sl, st=st, xsl=xsl: e.tensor_tensor(
            x2t[:, D * sl:D * (sl + 1)], xin2_ap(xsl),
            ytm[:, D * st:D * (st + 1)], ALU.add),
            reads=(f"xin2-{xsl}", "ytm"), writes=(f"x2t{sl}",))
        S.dma("gpsimd", x2_scr[st:st + 1, :].rearrange("o (p d) -> (o p) d",
                                                       p=128),
              x2t[:, D * sl:D * (sl + 1)], grp="x2scr",
              reads=(f"x2t{sl}",), writes=("x2_scr",))
        ln_front(f"x2t{sl}",
                 lambda sl=sl: x2t[:, D * sl:D * (sl + 1)], sl,
                 sq=xsq2, sq_res="xsq2")

    def ph2c_B(st):
        sl = st % 2
        ln_back(f"x2t{sl}",
                lambda sl=sl: x2t[:, D * sl:D * (sl + 1)],
                ln2tm[:, D * sl:D * (sl + 1)], sl, f"ln2tm{sl}")

    def ph2c_C(st):
        sl = st % 2

        def dst_fn(g, sl=sl):
            col0 = NHC * 128 * sl + 128 * (4 * g)
            ap = bass.AP(ln2sb, col0, [[2 * NHC * 128, 128], [128, 4], [1, 128]])
            return ap, f"ln2sb{sl}"
        tp16(f"ln2tm{sl}",
             lambda hc, sl=sl: ln2tm[:, D * sl + 128 * hc:D * sl + 128 * (hc + 1)],
             dst_fn, f"ln2tp{st}")
        S.dma("gpsimd",
              ln2_scr[st:st + 1, :].rearrange("o (p d) -> (o p) d", p=128),
              ln2sb[:, NHC * 128 * sl:NHC * 128 * (sl + 1)], grp="ln2scr",
              reads=(f"ln2sb{sl}",), writes=(f"ln2s{st}",))

    for t in range(18):
        if 1 <= t < 17:
            ph2c_B(t - 1)
        if t < 16:
            ph2c_A(t)
        if t == 16:
            emit_build(len(tiles) + 1)
            for csem in ("C0", "C1", "C2"):
                S._wait("sync", csem, 1)
                S._wait("gpsimd", csem, 1)
            emit_gfill(0, 0)
        if t >= 2:
            ph2c_C(t - 2)

    BS4 = (("B0", "B1", "B2", "B3"), ("Py", "Pd", "T0", "T1"))
    for rt in range(4):
        emit_ln2ts(rt)
        for i5 in range(NI5):
            iw = 512 if i5 < NI5 - 1 else ITER - 512 * (NI5 - 1)
            nib = iw // 128
            emit_gfill(rt, i5)
            for ib in range(nib):
                gb = f"B{ib}"
                ic = 4 * i5 + ib
                S.grp("tensor",
                      [lambda e, hc=hc, ib=ib, gb=gb: e.matmul(
                          bap(gb)[:, :],
                          grhs[:, 512 * hc + 128 * ib:512 * hc + 128 * (ib + 1)],
                          ln2ts[:, 512 * hc:512 * (hc + 1)],
                          start=(hc == 0), stop=(hc == NHC - 1))
                       for hc in range(NHC)],
                      reads=("ln2ts", "grhs", "grhsB"), writes=(gb,))
                S.cop("scalar", lambda e, ic=ic, gb=gb:
                      e.activation(hact[:, 512 * ic:512 * (ic + 1)],
                                   bap(gb)[:, :], AF.Silu),
                      reads=(gb,), writes=("hact",))
            emit_ufill(rt, i5)
            for ib in range(nib):
                ub = f"B{ib}"
                ic = 4 * i5 + ib
                S.grp("tensor",
                      [lambda e, hc=hc, ib=ib, ub=ub: e.matmul(
                          bap(ub)[:, :],
                          urhs[:, 512 * hc + 128 * ib:512 * hc + 128 * (ib + 1)],
                          ln2ts[:, 512 * hc:512 * (hc + 1)],
                          start=(hc == 0), stop=(hc == NHC - 1))
                       for hc in range(NHC)],
                      reads=("ln2ts", "urhs", "urhsB"), writes=(ub,))
                S.cop("vector", lambda e, ic=ic, ub=ub:
                      e.tensor_tensor(
                          hact[:, 512 * ic:512 * (ic + 1)],
                          hact[:, 512 * ic:512 * (ic + 1)],
                          bap(ub)[:, :], ALU.mult),
                      reads=(ub, "hact"), writes=("hact",))
        if _DBG and rt == 0:
            S.dma("scalar", dbg["d_hact0"].ap(), hact[:, :],
                  reads=("hact",))
        emit_ln2ts(rt + 1)
        emit_gfill(rt + 1, 0)
        emit_ufill(rt + 1, 0)
        for ph in range(2):
            for rb in range(4):
                st = rt * 4 + rb
                S.dma("sync",
                      x2rl[:, 1024 * rb:1024 * (rb + 1)],
                      bass.AP(x2_scr, st * (128 * D) + 1024 * ph,
                              [[D, 128], [1, 1024]]),
                      grp="x2r",
                      reads=("x2_scr",), writes=(f"x2r{rb}",))
            for ic in range(NIC):
                wsl = ic % 2
                wres = (f"wdrsA{wsl}", f"wdrsB{wsl}")
                S.dma("sync", wdrs[:, 1024 * wsl:1024 * wsl + 512],
                      d_full[128 * ic:128 * (ic + 1),
                             1024 * ph:1024 * ph + 512],
                      grp=("wd", wsl, 0),
                      reads=("d_full",), writes=(wres[0],))
                S.dma("gpsimd", wdrs[:, 1024 * wsl + 512:1024 * (wsl + 1)],
                      d_full[128 * ic:128 * (ic + 1),
                             1024 * ph + 512:1024 * (ph + 1)],
                      grp=("wd", wsl, 1),
                      reads=("d_full",), writes=(wres[1],))
                fns = []
                for rb in range(4):
                    for hh in range(2):
                        fns.append(lambda e, rb=rb, ic=ic, hh=hh, wsl=wsl,
                                   bk=BANKS8[2 * rb + hh]: e.matmul(
                                       bap(bk)[:, :],
                                       hact[:, 512 * ic + 128 * rb:
                                            512 * ic + 128 * (rb + 1)],
                                       wdrs[:, 1024 * wsl + 512 * hh:
                                            1024 * wsl + 512 * (hh + 1)],
                                       start=(ic == 0), stop=(ic == NIC - 1)))
                S.grp("tensor", fns,
                      reads=("hact",) + wres,
                      writes=tuple(BANKS8))
            for rb in range(4):
                st = rt * 4 + rb
                for hh in range(2):
                    bk = BANKS8[2 * rb + hh]
                    S.cop("vector", lambda e, rb=rb, hh=hh, bk=bk:
                          e.tensor_tensor(
                              x2rl[:, 1024 * rb + 512 * hh:
                                   1024 * rb + 512 * (hh + 1)],
                              x2rl[:, 1024 * rb + 512 * hh:
                                   1024 * rb + 512 * (hh + 1)],
                              bap(bk)[:, :], ALU.add),
                          reads=(bk, f"x2r{rb}"),
                          writes=(f"x2r{rb}",))
                S.dma("scalar",
                      y_out[128 * st:128 * (st + 1),
                            1024 * ph:1024 * (ph + 1)],
                      x2rl[:, 1024 * rb:1024 * (rb + 1)],
                      grp="yout",
                      reads=(f"x2r{rb}",))

    if _DBG:
        S.dma("gpsimd", dbg["d_x2"].ap(), x2_scr.ap(), reads=("x2_scr",))
        S.dma("gpsimd", dbg["d_ln2"].ap(), ln2_scr.ap(),
              reads=("ln2_scr",))
        S.dma("gpsimd", dbg["d_gfull"].ap(), g_full[0:128, :],
              reads=("g_full",))
        S.dma("gpsimd", dbg["d_dfull"].ap(), d_full[0:128, :],
              reads=("d_full",))

    S.drain_dmas("scalar", queues=("scalar",))
    S.drain_dmas("sync", queues=("sync",))
    S.drain_dmas("gpsimd", queues=("gpsimd",))
    S.verify()

    with nc.Block() as block:
        @block.sync
        def _(e):
            for item in S.q["sync"]:
                item[3](e)

        @block.scalar
        def _(e):
            for item in S.q["scalar"]:
                item[3](e)

        @block.vector
        def _(e):
            for item in S.q["vector"]:
                item[3](e)

        @block.tensor
        def _(e):
            for item in S.q["tensor"]:
                item[3](e)

        @block.gpsimd
        def _(e):
            for item in S.q["gpsimd"]:
                item[3](e)

    nc.reset()
    nc.finalize()
    return nc


_NC_CACHE = {}


def kernel(L0_attn_x, L0_attn_w_kv, L0_attn_w_q, L0_ffn_lut, L0_ffn_gate_idx,
           L0_ffn_up_idx, L0_ffn_down_idx):
    x = np.asarray(L0_attn_x, dtype=np.float32)
    w_kv = np.asarray(L0_attn_w_kv, dtype=np.float32)
    w_q = np.asarray(L0_attn_w_q, dtype=np.float32)
    lut = np.asarray(L0_ffn_lut, dtype=np.float32)
    g_i = np.asarray(L0_ffn_gate_idx, dtype=np.int32)
    u_i = np.asarray(L0_ffn_up_idx, dtype=np.int32)
    d_i = np.asarray(L0_ffn_down_idx, dtype=np.int32)

    if "nc" not in _NC_CACHE:
        _NC_CACHE["nc"] = build_nc()
    nc = _NC_CACHE["nc"]

    wqT_h = np.ascontiguousarray(w_q.T)
    wkvT_h = np.ascontiguousarray(w_kv.T)
    lutb_h = np.ascontiguousarray(np.broadcast_to(lut[None, :], (128, 16)))
    gT = g_i.T
    uT = u_i.T
    dT = d_i.T

    in_maps = make_in_maps(x, wqT_h, wkvT_h, lutb_h, gT, uT, dT)
    res = run_bass_kernel_spmd(nc, in_maps, core_ids=list(range(8)))
    out = np.empty((4, T, D), dtype=np.float32)
    for c in range(8):
        b, half = c // 2, c % 2
        yo = np.asarray(res.results[c]["y_out"]).reshape(16, 128, D)
        for j in range(16):
            r0 = 128 * (2 * j + half)
            out[b, r0:r0 + 128, :] = yo[j]
    return out


def make_in_maps(x, wqT_h, wkvT_h, lutb_h, gT, uT, dT):
    in_maps = []
    for c in range(8):
        b, half = c // 2, c % 2
        xb = x[b].reshape(32, 128, D)
        own = np.ascontiguousarray(
            xb[[2 * j + half for j in range(16)]]).reshape(2048, D)
        oth = np.ascontiguousarray(
            xb[[2 * j + 1 - half for j in range(16)]]).reshape(2048, D)
        mb = np.full((128, 128), float(half), dtype=ml_dtypes.bfloat16)
        in_maps.append({
            "xq": own, "xo": oth, "wqT": wqT_h, "wkvT": wkvT_h, "lutb": lutb_h,
            "gidx": np.ascontiguousarray(gT[256 * c:256 * (c + 1)]),
            "uidx": np.ascontiguousarray(uT[256 * c:256 * (c + 1)]),
            "didx": np.ascontiguousarray(dT[688 * c:688 * (c + 1)]),
            "maskb": mb,
        })
    return in_maps
